# version 09
# baseline (speedup 1.0000x reference)
import numpy as np
import ml_dtypes
from contextlib import ExitStack

import concourse.bass as bass
import concourse.mybir as mybir
from concourse.bass_utils import run_bass_kernel_spmd

F32 = mybir.dt.float32
BF16 = mybir.dt.bfloat16
U16 = mybir.dt.uint16
U32 = mybir.dt.uint32
U8 = mybir.dt.uint8
ALU = mybir.AluOpType
AF = mybir.ActivationFunctionType

B, S, D, H, E, NM = 4, 4096, 2048, 16, 128, 16384
TC = 2048
TT = 512
NTT = TC // TT
NCH = D // 128
NTN = NM // 128
NGR = 8
NPG = NTN // NGR
SCALE_QK = 1.0 / float(np.sqrt(128.0))
EXP_SHIFT = -25.0
K_G1 = 0.7978845608028654
K_G2 = K_G1 * 0.044715

ISZ = {F32: 4, BF16: 2, U16: 2, U8: 1}


class Ctr:
    def __init__(self, h):
        self.h = h
        self.n = 0


def _carve(arena, off, p, cols, dt):
    nb = cols * ISZ[dt]
    assert off % 64 == 0 and off + nb <= 134 * 1024, (off, nb)
    return arena[0:p, off:off + nb].bitcast(dt)


def build_program(debug=0):
    nc = bass.Bass("TRN2", target_bir_lowering=False)

    x_in = nc.dram_tensor("x", [128, NCH, TC], F32, kind="ExternalInput").ap()
    wq_in = nc.dram_tensor("wq", [H, 128, NCH, E], F32, kind="ExternalInput").ap()
    wk_in = nc.dram_tensor("wk", [H, 128, NCH, E], F32, kind="ExternalInput").ap()
    wv_in = nc.dram_tensor("wv", [H, 128, NCH, E], F32, kind="ExternalInput").ap()
    wo_in = nc.dram_tensor("wo", [128, H, D], F32, kind="ExternalInput").ap()
    u16g = nc.dram_tensor("u16g", [NCH, NGR, 128, 512], U16, kind="ExternalInput").ap()
    u16u = nc.dram_tensor("u16u", [NCH, NGR, 128, 512], U16, kind="ExternalInput").ap()
    u16d = nc.dram_tensor("u16d", [128, 128, 512], U16, kind="ExternalInput").ap()
    sgT = nc.dram_tensor("sgT", [64, NM], F32, kind="ExternalInput").ap()
    suT = nc.dram_tensor("suT", [64, NM], F32, kind="ExternalInput").ap()
    sdT = nc.dram_tensor("sdT", [512, D], F32, kind="ExternalInput").ap()
    kvec_in = nc.dram_tensor("kvec", [128, 8], F32, kind="ExternalInput").ap()

    xout = nc.dram_tensor("xout", [128, NCH, TC], F32, kind="ExternalOutput").ap()

    k1 = "Internal" if debug < 1 else "ExternalOutput"
    k2 = "Internal" if debug < 2 else "ExternalOutput"
    wg_d = nc.dram_tensor("wg_d", [NTN, 128, NCH, 128], BF16, kind=k2).ap()
    wu_d = nc.dram_tensor("wu_d", [NTN, 128, NCH, 128], BF16, kind=k2).ap()
    wd_d = nc.dram_tensor("wd_d", [NGR, 4, 128, NPG, 512], BF16, kind=k2).ap()
    x1_d = nc.dram_tensor("x1_d", [128, NCH, TC], F32, kind=k1).ap()
    x2_d = nc.dram_tensor("x2_d", [128, NCH, TC], F32, kind=k1).ap()
    qf_d = nc.dram_tensor("qf_d", [H, E, TC], BF16, kind=k2).ap()
    kv_loc = nc.dram_tensor("kv_loc", [2 * D, TC], BF16, kind="Internal").ap()
    kv_ag = nc.dram_tensor("kv_ag", [4 * D, TC], BF16, kind="Internal").ap()
    if debug >= 2:
        kvl_dump = nc.dram_tensor("kvl_dump", [2 * D, TC], BF16, kind="ExternalOutput").ap()
        kvag_dump = nc.dram_tensor("kvag_dump", [4 * D, TC], BF16, kind="ExternalOutput").ap()

    with ExitStack() as ctx:
        EN = ctx.enter_context

        xres = EN(nc.sbuf_tensor("xres", [128, NCH * TT], F32))
        xln = EN(nc.sbuf_tensor("xln", [128, NCH * TT], BF16))
        masks = EN(nc.sbuf_tensor("masks", [128, 8 * TT], BF16))
        iq = EN(nc.sbuf_tensor("iq", [128, TT], F32))
        kvec = EN(nc.sbuf_tensor("kvec_sb", [128, 8], F32))
        pmi = EN(nc.sbuf_tensor("pmi", [128, 128], F32))
        identf = EN(nc.sbuf_tensor("identf", [128, 128], F32))
        ones_col = EN(nc.sbuf_tensor("ones_col", [128, 1], BF16))
        ones1 = EN(nc.sbuf_tensor("ones1", [1, 128], BF16))
        c_m25 = EN(nc.sbuf_tensor("c_m25", [128, 1], F32))
        c_eps = EN(nc.sbuf_tensor("c_eps", [1, 1], F32))
        rows = EN(nc.sbuf_tensor("rows", [1, 1536], F32))
        rowsb = EN(nc.sbuf_tensor("rowsb", [1, 1024], BF16))
        arena = EN(nc.sbuf_tensor("arena", [128, 134 * 1024], U8))

        u16st = [_carve(arena, 0 + i * 1024, 128, 512, U16) for i in range(4)]
        sclgf = [[_carve(arena, 4096 + (2 * m + i) * 8192, 64, 2048, F32)
                  for i in range(2)] for m in range(3)]
        sclgt = [[_carve(arena, 53248 + (2 * m + i) * 4096, 64, 2048, BF16)
                  for i in range(2)] for m in range(3)]
        sclf = [_carve(arena, 77824 + i * 4096, 128, 2048, BF16) for i in range(4)]
        xfull = _carve(arena, 94208, 128, 2048, U16)
        dsg = _carve(arena, 98304, 128, 2048, U16)
        dt1 = _carve(arena, 102400, 128, 2048, BF16)
        decb = [_carve(arena, 106496 + i * 4096, 128, 2048, BF16) for i in range(4)]

        wgb = [_carve(arena, 0 + i * 4096, 128, 2048, BF16) for i in range(2)]
        wub = [_carve(arena, 8192 + i * 4096, 128, 2048, BF16) for i in range(2)]
        wdb = [_carve(arena, 16384 + i * 16384, 128, 8192, BF16) for i in range(2)]
        inter = _carve(arena, 49152, 128, NPG * TT, BF16)
        dacc = _carve(arena, 65536, 128, 4 * D, F32)
        xball = _carve(arena, 98304, 128, NCH * TT, BF16)
        sqs = [_carve(arena, 114688 + i * 1024, 128, 512, BF16) for i in range(2)]
        gsq = [_carve(arena, 116736 + i * 2048, 128, 512, F32) for i in range(2)]
        gav = _carve(arena, 120832, 128, 512, F32)
        ginn = [_carve(arena, 122880 + i * 2048, 128, 512, F32) for i in range(2)]
        gtha = [_carve(arena, 126976 + i * 2048, 128, 512, F32) for i in range(2)]
        uh = [_carve(arena, 131072 + i * 2048, 128, 512, F32) for i in range(2)]
        mb_sb = _carve(arena, 135168, 128, 512, BF16)
        rb_sb = _carve(arena, 136192, 128, 512, BF16)

        wqb = [_carve(arena, 0 + i * 4096, 128, 2048, BF16) for i in range(2)]
        wkb = [_carve(arena, 8192 + i * 4096, 128, 2048, BF16) for i in range(2)]
        wvb = [_carve(arena, 16384 + i * 4096, 128, 2048, BF16) for i in range(2)]
        qkt = [_carve(arena, 24576 + i * 1024, 128, 512, BF16) for i in range(6)]

        qsb = [_carve(arena, 0 + i * 1024, 128, 512, BF16) for i in range(2)]
        ksb = [_carve(arena, 2048 + i * 8192, 128, 4096, BF16) for i in range(2)]
        vsb = [_carve(arena, 18432 + i * 8192, 128, 4096, BF16) for i in range(2)]
        ptr_ = [_carve(arena, 64512 + i * 1024, 128, 512, BF16) for i in range(4)]
        ptm = [_carve(arena, 68608 + i * 1024, 128, 512, BF16) for i in range(4)]
        zall = _carve(arena, 38912, 128, H * TT, BF16)
        wob = [_carve(arena, 55296 + i * 4096, 128, 2048, BF16) for i in range(2)]
        rb2 = _carve(arena, 63488, 128, 512, BF16)

        bank = [EN(nc.psum_tensor(f"bk{i}", [128, 512], F32)) for i in range(8)]

        names = ["g0", "v0", "d_u0", "d_u1", "d_u2", "d_u3", "d_s0", "d_s1", "d_s2", "d_s3", "d_st0", "d_st1", "d_st2", "d_st3", "d_sc", "d_ct", "d_v",
                 "f_x", "f_lnv", "f_lnmm", "f_st1", "f_st2", "f_st3",
                 "f_bc", "f_ap", "f_w", "f_gmm", "f_sqa", "f_inn",
                 "f_tha", "f_epi", "f_dw", "f_dpe", "f_dadd", "f_tp",
                 "f_res", "f_xs",
                 "q_w", "q_mm", "q_ev", "q_st0", "q_st1", "c_ag",
                 "a_ld", "a_s", "a_exp", "a_msk", "a_pv", "a_den",
                 "a_rb", "a_rbe", "a_z", "a_wo", "a_omm", "a_res"]
        C = {nm: Ctr(EN(nc.semaphore(nm))) for nm in names}

        Q = {"sy": [], "gp": [], "ve": [], "sc": [], "te": []}

        def em(qn, fn):
            Q[qn].append(fn)

        def wt(qn, ctr, val):
            if val > 0:
                em(qn, lambda e, s=ctr.h, v=val: e.wait_ge(s, v))

        em("gp", lambda g: g.dma_start(out=kvec[:], in_=kvec_in).then_inc(C["g0"].h, 16))
        C["g0"].n += 16
        em("gp", lambda g: g.iota(iq[:], pattern=[[1, TT]], base=0, channel_multiplier=0,
                                  allow_small_or_imprecise_dtypes=True).then_inc(C["g0"].h, 1))
        C["g0"].n += 1
        em("gp", lambda g: g.iota(pmi[:], pattern=[[-1, 128]], base=0, channel_multiplier=1,
                                  allow_small_or_imprecise_dtypes=True).then_inc(C["g0"].h, 1))
        C["g0"].n += 1
        wt("ve", C["g0"], C["g0"].n)
        for i in range(8):
            em("ve", lambda v, i=i: v.tensor_scalar(
                masks[:, i * TT:(i + 1) * TT], iq[:], kvec[:, i:i + 1], None, op0=ALU.is_ge))
        em("ve", lambda v: v.tensor_scalar(identf[:], pmi[:], 0.0, None, op0=ALU.is_equal))
        em("ve", lambda v: v.memset(ones_col[:], 1.0))
        em("ve", lambda v: v.memset(ones1[:], 1.0))
        em("ve", lambda v: v.memset(c_m25[:], EXP_SHIFT))
        em("ve", lambda v: v.memset(c_eps[:], 1e-5).then_inc(C["v0"].h, 1))
        C["v0"].n += 1
        wt("sc", C["v0"], 1)
        wt("te", C["v0"], 1)

        blocks = []
        for G in range(NGR):
            blocks += [("g", c, G) for c in range(NCH)]
            blocks += [("u", c, G) for c in range(NCH)]
            blocks += [("d", G * NPG + j, G) for j in range(NPG)]
        NBLK = len(blocks)

        def emit_scale_loads(GL):
            if GL >= NGR:
                return
            if GL >= 2:
                wt("gp", C["d_ct"], 3 * (GL - 1))
            for m_idx, ssrc in enumerate((
                    sgT[:, GL * 2048:(GL + 1) * 2048],
                    suT[:, GL * 2048:(GL + 1) * 2048],
                    sdT[64 * GL:64 * (GL + 1), :])):
                em("gp", lambda e, o=sclgf[m_idx][GL % 2], i=ssrc:
                   e.dma_start(out=o, in_=i).then_inc(C["d_sc"].h, 16))
                C["d_sc"].n += 16

        def emit_scale_casts(G):
            for m_idx in range(3):
                wt("ve", C["d_sc"], 16 * (3 * G + m_idx + 1))
                if G >= 2 and m_idx == 0:
                    for _sn in range(4):
                        base = 48 * (G - 1)
                        wt("ve", C[f"d_s{_sn}"], 16 * (base // 4 + (1 if _sn < base % 4 else 0)))
                em("ve", lambda v, m=m_idx, G=G: v.tensor_copy(
                    sclgt[m][G % 2][:], sclgf[m][G % 2][:]).then_inc(C["d_ct"].h, 1))
                C["d_ct"].n += 1

        for k, (m, a, G) in enumerate(blocks):
            m_idx = {"g": 0, "u": 1, "d": 2}[m]
            if k == 0:
                emit_scale_loads(0)
                emit_scale_loads(1)
                emit_scale_casts(0)
            elif k % 48 == 0:
                emit_scale_loads(G + 1)
                emit_scale_casts(G)
            if k >= 4:
                wt("sy", C["d_v"], k - 3)
            src = (u16g[a, G] if m == "g" else u16u[a, G] if m == "u" else u16d[a])
            em("sy", lambda e, o=u16st[k % 4], i=src, k=k:
               e.dma_start(out=o, in_=i).then_inc(C[f"d_u{k % 4}"].h, 16))
            C[f"d_u{k % 4}"].n += 16
            c_in = a % NPG if m == "d" else a
            wt("gp", C["d_ct"], 3 * G + m_idx + 1)
            if k >= 4:
                wt("gp", C["d_v"], k - 3)
            em("gp", lambda e, o=sclf[k % 4], i=sclgt[m_idx][G % 2][4 * c_in:4 * c_in + 4, :], k=k:
               e.dma_start(out=o, in_=i.unsqueeze(1).broadcast_to([4, 32, 2048])).then_inc(C[f"d_s{k % 4}"].h, 16))
            C[f"d_s{k % 4}"].n += 16
            wt("ve", C[f"d_u{k % 4}"], 16 * (k // 4 + 1))
            u32s = u16st[k % 4].bitcast(U32)
            xf32 = xfull.bitcast(U32)
            sg32v = dsg.bitcast(U32)
            em("ve", lambda v, u=u32s: v.tensor_scalar(xf32[:, 0:256], u, 6, 0x01C001C0,
                                                       op0=ALU.logical_shift_left, op1=ALU.bitwise_and))
            em("ve", lambda v, u=u32s: v.tensor_scalar(xf32[:, 256:512], u, 2, 0x01C001C0,
                                                       op0=ALU.logical_shift_left, op1=ALU.bitwise_and))
            em("ve", lambda v, u=u32s: v.tensor_scalar(xf32[:, 512:768], u, 2, 0x01C001C0,
                                                       op0=ALU.logical_shift_right, op1=ALU.bitwise_and))
            em("ve", lambda v, u=u32s: v.tensor_scalar(xf32[:, 768:1024], u, 6, 0x01C001C0,
                                                       op0=ALU.logical_shift_right, op1=ALU.bitwise_and))
            em("ve", lambda v, u=u32s: v.tensor_scalar(sg32v[:, 0:256], u, 12, 0x80008000,
                                                       op0=ALU.logical_shift_left, op1=ALU.bitwise_and))
            em("ve", lambda v, u=u32s: v.tensor_scalar(sg32v[:, 256:512], u, 8, 0x80008000,
                                                       op0=ALU.logical_shift_left, op1=ALU.bitwise_and))
            em("ve", lambda v, u=u32s: v.tensor_scalar(sg32v[:, 512:768], u, 4, 0x80008000,
                                                       op0=ALU.logical_shift_left, op1=ALU.bitwise_and))
            em("ve", lambda v, u=u32s: v.tensor_scalar(sg32v[:, 768:1024], u, 0, 0x80008000,
                                                       op0=ALU.logical_shift_left, op1=ALU.bitwise_and))
            em("ve", lambda v: v.tensor_scalar(xfull[:], xfull[:], 0x3F00, None, op0=ALU.add))
            em("ve", lambda v: v.tensor_scalar(dt1[:], xfull[:].bitcast(BF16), 2.0, -1.0,
                                               op0=ALU.mult, op1=ALU.add))
            em("ve", lambda v: v.tensor_tensor(dt1[:], xfull[:].bitcast(BF16), dt1[:], op=ALU.min))
            wt("ve", C[f"d_s{k % 4}"], 16 * (k // 4 + 1))
            em("ve", lambda v, s=sclf[k % 4]: v.tensor_tensor(
                dsg[:].bitcast(U32), s.bitcast(U32), dsg[:].bitcast(U32), op=ALU.bitwise_or))
            if k >= 4:
                wt("ve", C[f"d_st{k % 4}"], 16 * (k // 4))
            em("ve", lambda v, o=decb[k % 4]: v.tensor_tensor(
                o, dt1[:], dsg[:].bitcast(BF16), op=ALU.mult).then_inc(C["d_v"].h, 1))
            C["d_v"].n += 1
            wt("sc", C["d_v"], k + 1)
            if m in ("g", "u"):
                dst = (wg_d if m == "g" else wu_d)[G * NPG:(G + 1) * NPG, :, a, :] \
                    .rearrange("a p b -> p a b")
            else:
                dst = wd_d[a // NPG, :, :, a % NPG, :].rearrange("a p b -> p a b")
            em("sc", lambda e, o=dst, i=decb[k % 4], k=k:
               e.dma_start(out=o, in_=i).then_inc(C[f"d_st{k % 4}"].h, 16))
            C[f"d_st{k % 4}"].n += 16

        def ln_tile():
            wt("ve", C["f_x"], C["f_x"].n)
            wt("ve", C["f_lnmm"], C["f_lnmm"].n)
            for c in range(NCH):
                em("ve", lambda v, c=c: v.tensor_copy(
                    xball[:, c * TT:(c + 1) * TT], xres[:, c * TT:(c + 1) * TT]))
                if C["f_lnmm"].n >= 1:
                    wt("ve", C["f_lnmm"], C["f_lnmm"].n - 1)
                em("ve", lambda v, c=c: v.tensor_tensor(
                    sqs[c % 2], xball[:, c * TT:(c + 1) * TT],
                    xball[:, c * TT:(c + 1) * TT], op=ALU.mult).then_inc(C["f_lnv"].h, 1))
                C["f_lnv"].n += 1
                wt("te", C["f_lnv"], C["f_lnv"].n)
                if c == 0:
                    wt("te", C["f_res"], C["f_res"].n)
                    wt("te", C["a_res"], C["a_res"].n)
                em("te", lambda t, c=c: t.matmul(
                    bank[6][0:1, :], ones_col[:], xball[:, c * TT:(c + 1) * TT],
                    start=(c == 0), stop=(c == NCH - 1)))
                em("te", lambda t, c=c: t.matmul(
                    bank[7][0:1, :], ones_col[:], sqs[c % 2],
                    start=(c == 0), stop=(c == NCH - 1)).then_inc(C["f_lnmm"].h, 1))
                C["f_lnmm"].n += 1
            wt("ve", C["f_lnmm"], C["f_lnmm"].n)
            em("ve", lambda v: v.tensor_scalar(rows[:, 0:512], bank[6][0:1, :], 1.0 / 2048.0, None, op0=ALU.mult))
            em("ve", lambda v: v.tensor_scalar(rows[:, 512:1024], bank[7][0:1, :], 1.0 / 2048.0, None, op0=ALU.mult))
            em("ve", lambda v: v.tensor_tensor(rows[:, 1024:1536], rows[:, 0:512], rows[:, 0:512], op=ALU.mult))
            em("ve", lambda v: v.tensor_tensor(
                rows[:, 512:1024], rows[:, 512:1024], rows[:, 1024:1536], op=ALU.subtract)
                .then_inc(C["f_st1"].h, 1))
            C["f_st1"].n += 1
            wt("sc", C["f_st1"], C["f_st1"].n)
            em("sc", lambda s: s.activation(rows[:, 1024:1536], rows[:, 512:1024], AF.Sqrt,
                                            bias=c_eps[:], scale=1.0).then_inc(C["f_st2"].h, 1))
            C["f_st2"].n += 1
            wt("ve", C["f_st2"], C["f_st2"].n)
            em("ve", lambda v: v.reciprocal(rows[:, 512:1024], rows[:, 1024:1536]))
            em("ve", lambda v: v.tensor_copy(rowsb[:, 0:512], rows[:, 0:512]))
            em("ve", lambda v: v.tensor_copy(rowsb[:, 512:1024], rows[:, 512:1024]).then_inc(C["f_st3"].h, 1))
            C["f_st3"].n += 1
            wt("te", C["f_st3"], C["f_st3"].n)
            em("te", lambda t: t.matmul(bank[6][:], ones1[:], rowsb[:, 0:512], start=True, stop=True))
            em("te", lambda t: t.matmul(bank[7][:], ones1[:], rowsb[:, 512:1024], start=True, stop=True)
               .then_inc(C["f_bc"].h, 1))
            C["f_bc"].n += 1
            wt("ve", C["f_bc"], C["f_bc"].n)
            em("ve", lambda v: v.tensor_copy(mb_sb[:], bank[6][:]))
            em("ve", lambda v: v.tensor_copy(rb_sb[:], bank[7][:]))
            wt("ve", C["f_gmm"], C["f_gmm"].n)
            wt("ve", C["q_mm"], C["q_mm"].n)
            for c in range(NCH):
                em("ve", lambda v, c=c: v.tensor_tensor(
                    xball[:, c * TT:(c + 1) * TT], xball[:, c * TT:(c + 1) * TT], mb_sb[:], op=ALU.subtract))
                em("ve", lambda v, c=c: v.tensor_tensor(
                    xln[:, c * TT:(c + 1) * TT], xball[:, c * TT:(c + 1) * TT], rb_sb[:], op=ALU.mult)
                    .then_inc(C["f_ap"].h, 1))
                C["f_ap"].n += 1

        def ffn_pass(src_ap, dst_ap):
            for ti in range(NTT):
                wt("gp", C["f_xs"], C["f_xs"].n)
                em("gp", lambda e, i=src_ap[:, :, ti * TT:(ti + 1) * TT]:
                   e.dma_start(out=xres[:], in_=i).then_inc(C["f_x"].h, 16))
                C["f_x"].n += 16
                ln_tile()
                for g in range(NGR):
                    for j in range(NPG):
                        jn = g * NPG + j
                        sl = jn % 2
                        if C["f_gmm"].n >= 1:
                            wt("sy", C["f_gmm"], C["f_gmm"].n - 1)
                        em("sy", lambda e, i=wg_d[jn], o=wgb[sl]: e.dma_start(
                            out=o, in_=i).then_inc(C["f_w"].h, 16))
                        C["f_w"].n += 16
                        em("sy", lambda e, i=wu_d[jn], o=wub[sl]: e.dma_start(
                            out=o, in_=i).then_inc(C["f_w"].h, 16))
                        C["f_w"].n += 16
                        wt("te", C["f_w"], C["f_w"].n)
                        wt("te", C["f_ap"], C["f_ap"].n)
                        if C["f_epi"].n >= 1:
                            wt("te", C["f_epi"], C["f_epi"].n - 1)
                        for c in range(NCH):
                            em("te", lambda t, c=c, sl=sl: t.matmul(
                                bank[sl][:], wgb[sl][:, c * 128:(c + 1) * 128],
                                xln[:, c * TT:(c + 1) * TT], start=(c == 0), stop=(c == NCH - 1)))
                        for c in range(NCH):
                            if c == NCH - 1:
                                em("te", lambda t, c=c, sl=sl: t.matmul(
                                    bank[2 + sl][:], wub[sl][:, c * 128:(c + 1) * 128],
                                    xln[:, c * TT:(c + 1) * TT], start=False, stop=True)
                                    .then_inc(C["f_gmm"].h, 1))
                            else:
                                em("te", lambda t, c=c, sl=sl: t.matmul(
                                    bank[2 + sl][:], wub[sl][:, c * 128:(c + 1) * 128],
                                    xln[:, c * TT:(c + 1) * TT], start=(c == 0), stop=False))
                        C["f_gmm"].n += 1
                        wt("sc", C["f_gmm"], C["f_gmm"].n)
                        em("sc", lambda s, sl=sl: s.activation(gsq[sl], bank[sl][:], AF.Square))
                        em("sc", lambda s, sl=sl: s.activation(uh[sl], bank[2 + sl][:], AF.Copy, scale=0.5)
                           .then_inc(C["f_sqa"].h, 1))
                        C["f_sqa"].n += 1
                        wt("ve", C["f_sqa"], C["f_sqa"].n)
                        if g == 0 and j == 0:
                            wt("ve", C["f_dpe"], C["f_dpe"].n)
                            wt("ve", C["a_omm"], C["a_omm"].n)
                        if j == 0:
                            wt("ve", C["f_dpe"], C["f_dpe"].n)
                        em("ve", lambda v, sl=sl: v.tensor_scalar(gav[:], gsq[sl], K_G2, K_G1,
                                                                  op0=ALU.mult, op1=ALU.add))
                        em("ve", lambda v, sl=sl: v.tensor_tensor(ginn[sl], gav[:], bank[sl][:], op=ALU.mult)
                           .then_inc(C["f_inn"].h, 1))
                        C["f_inn"].n += 1
                        wt("sc", C["f_inn"], C["f_inn"].n)
                        em("sc", lambda s, sl=sl: s.activation(gtha[sl], ginn[sl], AF.Tanh)
                           .then_inc(C["f_tha"].h, 1))
                        C["f_tha"].n += 1
                        wt("ve", C["f_tha"], C["f_tha"].n)
                        em("ve", lambda v, sl=sl: v.tensor_tensor(gav[:], bank[sl][:], uh[sl], op=ALU.mult))
                        em("ve", lambda v, sl=sl, j=j: v.scalar_tensor_tensor(
                            inter[:, j * TT:(j + 1) * TT], gtha[sl], 1.0, gav[:],
                            op0=ALU.add, op1=ALU.mult).then_inc(C["f_epi"].h, 1))
                        C["f_epi"].n += 1
                    for dt in range(4):
                        dsl = dt % 2
                        if C["f_dpe"].n >= 4:
                            wt("sy", C["f_dpe"], C["f_dpe"].n - 4)
                        em("sy", lambda e, i=wd_d[g, dt], o=wdb[dsl]: e.dma_start(
                            out=o, in_=i).then_inc(C["f_dw"].h, 16))
                        C["f_dw"].n += 16
                        wt("te", C["f_dw"], C["f_dw"].n)
                        wt("te", C["f_epi"], C["f_epi"].n)
                        for tsub in range(4):
                            bsl = 4 + (tsub % 2)
                            if C["f_dadd"].n >= 1:
                                wt("te", C["f_dadd"], C["f_dadd"].n - 1)
                            for j in range(NPG):
                                if j == NPG - 1:
                                    em("te", lambda t, j=j, tsub=tsub, dsl=dsl, bsl=bsl: t.matmul(
                                        bank[bsl][:],
                                        inter[:, j * TT + tsub * 128: j * TT + tsub * 128 + 128],
                                        wdb[dsl][:, j * 512:(j + 1) * 512],
                                        start=False, stop=True).then_inc(C["f_dpe"].h, 1))
                                else:
                                    em("te", lambda t, j=j, tsub=tsub, dsl=dsl, bsl=bsl: t.matmul(
                                        bank[bsl][:],
                                        inter[:, j * TT + tsub * 128: j * TT + tsub * 128 + 128],
                                        wdb[dsl][:, j * 512:(j + 1) * 512],
                                        start=(j == 0), stop=False))
                            C["f_dpe"].n += 1
                            wt("ve", C["f_dpe"], C["f_dpe"].n)
                            if g == 0:
                                wt("ve", C["f_tp"], C["f_tp"].n)
                            dslice = dacc[:, tsub * D + dt * 512: tsub * D + dt * 512 + 512]
                            if g == 0:
                                em("ve", lambda v, o=dslice, b=bank[bsl]: v.tensor_copy(o, b[:])
                                   .then_inc(C["f_dadd"].h, 1))
                            else:
                                em("ve", lambda v, o=dslice, b=bank[bsl]: v.tensor_tensor(
                                    o, o, b[:], op=ALU.add).then_inc(C["f_dadd"].h, 1))
                            C["f_dadd"].n += 1
                wt("te", C["f_dadd"], C["f_dadd"].n)
                wt("te", C["f_ap"], C["f_ap"].n)
                for tsub in range(4):
                    for c in range(NCH):
                        tb_ = 6 + ((tsub * NCH + c) % 2)
                        if C["f_tp"].n >= 2:
                            wt("te", C["f_res"], C["f_tp"].n - 1)
                        em("te", lambda t, tsub=tsub, c=c, tb_=tb_: t.transpose(
                            bank[tb_][:, 0:128],
                            dacc[:, tsub * D + c * 128: tsub * D + c * 128 + 128], identf[:])
                            .then_inc(C["f_tp"].h, 1))
                        C["f_tp"].n += 1
                        wt("ve", C["f_tp"], C["f_tp"].n)
                        xslice = xres[:, c * TT + tsub * 128: c * TT + tsub * 128 + 128]
                        em("ve", lambda v, x=xslice, tb_=tb_: v.scalar_tensor_tensor(
                            x, bank[tb_][:, 0:128], 0.5, x, op0=ALU.mult, op1=ALU.add)
                            .then_inc(C["f_res"].h, 1))
                        C["f_res"].n += 1
                wt("gp", C["f_res"], C["f_res"].n)
                em("gp", lambda e, o=dst_ap[:, :, ti * TT:(ti + 1) * TT]:
                   e.dma_start(out=o, in_=xres[:]).then_inc(C["f_xs"].h, 16))
                C["f_xs"].n += 16

        def p3():
            wt("gp", C["f_gmm"], C["f_gmm"].n)
            wt("gp", C["f_dpe"], C["f_dpe"].n)
            wt("ve", C["f_dpe"], C["f_dpe"].n)
            qmm_hist = []
            for ti in range(NTT):
                wt("gp", C["f_xs"], C["f_xs"].n)
                wt("gp", C["f_ap"], C["f_ap"].n)
                em("gp", lambda e, i=x1_d[:, :, ti * TT:(ti + 1) * TT]:
                   e.dma_start(out=xres[:], in_=i).then_inc(C["f_x"].h, 16))
                C["f_x"].n += 16
                ln_tile()
                for h in range(H):
                    hsl = h % 2
                    gi = ti * H + h
                    if gi >= 2:
                        wt("gp", C["q_mm"], qmm_hist[gi - 2])
                    for (w_in, buf) in ((wq_in, wqb), (wk_in, wkb), (wv_in, wvb)):
                        em("gp", lambda e, i=w_in[h], o=buf[hsl]: e.dma_start(
                            out=o, in_=i).then_inc(C["q_w"].h, 16))
                        C["q_w"].n += 16
                    wt("te", C["q_w"], C["q_w"].n)
                    wt("te", C["f_ap"], C["f_ap"].n)
                    if gi >= 2:
                        wt("te", C["q_ev"], C["q_ev"].n - 1)
                    for c in range(NCH):
                        em("te", lambda t, c=c, hsl=hsl: t.matmul(
                            bank[3 * hsl][:], wqb[hsl][:, c * 128:(c + 1) * 128],
                            xln[:, c * TT:(c + 1) * TT], start=(c == 0), stop=(c == NCH - 1)))
                    for c in range(NCH):
                        em("te", lambda t, c=c, hsl=hsl: t.matmul(
                            bank[3 * hsl + 1][:], wkb[hsl][:, c * 128:(c + 1) * 128],
                            xln[:, c * TT:(c + 1) * TT], start=(c == 0), stop=(c == NCH - 1)))
                    for c in range(NCH):
                        for ks in range(4):
                            last = (c == NCH - 1 and ks == 3)
                            if last:
                                em("te", lambda t, c=c, ks=ks, hsl=hsl: t.matmul(
                                    bank[3 * hsl + 2][:, ks * 128:(ks + 1) * 128],
                                    xln[:, c * TT + ks * 128: c * TT + ks * 128 + 128],
                                    wvb[hsl][:, c * 128:(c + 1) * 128],
                                    start=False, stop=True).then_inc(C["q_mm"].h, 1))
                            else:
                                em("te", lambda t, c=c, ks=ks, hsl=hsl: t.matmul(
                                    bank[3 * hsl + 2][:, ks * 128:(ks + 1) * 128],
                                    xln[:, c * TT + ks * 128: c * TT + ks * 128 + 128],
                                    wvb[hsl][:, c * 128:(c + 1) * 128],
                                    start=(c == 0 and ks == 0), stop=False))
                    C["q_mm"].n += 1
                    qmm_hist.append(C["q_mm"].n)
                    wt("ve", C["q_mm"], C["q_mm"].n)
                    if gi >= 2:
                        wt("ve", C[f"q_st{gi % 2}"], 48 * (gi // 2))
                    em("ve", lambda v, hsl=hsl: v.tensor_copy(qkt[3 * hsl][:], bank[3 * hsl][:]))
                    em("ve", lambda v, hsl=hsl: v.tensor_copy(qkt[3 * hsl + 1][:], bank[3 * hsl + 1][:]))
                    em("ve", lambda v, hsl=hsl: v.tensor_copy(qkt[3 * hsl + 2][:], bank[3 * hsl + 2][:])
                       .then_inc(C["q_ev"].h, 1))
                    C["q_ev"].n += 1
                    wt("sy", C["q_ev"], C["q_ev"].n)
                    em("sy", lambda e, o=qf_d[h, :, ti * TT:(ti + 1) * TT], i=qkt[3 * hsl][:], gi=gi:
                       e.dma_start(out=o, in_=i).then_inc(C[f"q_st{gi % 2}"].h, 16))
                    C[f"q_st{gi % 2}"].n += 16
                    em("sy", lambda e, o=kv_loc[h * 128:(h + 1) * 128, ti * TT:(ti + 1) * TT],
                       i=qkt[3 * hsl + 1][:], gi=gi: e.dma_start(out=o, in_=i).then_inc(C[f"q_st{gi % 2}"].h, 16))
                    C[f"q_st{gi % 2}"].n += 16
                    vdst = kv_loc[D + ti * TT: D + (ti + 1) * TT, h * 128:(h + 1) * 128] \
                        .rearrange("(a p) e -> p a e", p=128)
                    em("sy", lambda e, o=vdst, i=qkt[3 * hsl + 2][:], gi=gi:
                       e.dma_start(out=o, in_=i).then_inc(C[f"q_st{gi % 2}"].h, 16))
                    C[f"q_st{gi % 2}"].n += 16

        def p4():
            wt("gp", C["q_st0"], C["q_st0"].n)
            wt("gp", C["q_st1"], C["q_st1"].n)
            em("gp", lambda g: g.collective_compute(
                "AllGather", ALU.bypass,
                replica_groups=[[0, 1], [2, 3], [4, 5], [6, 7]],
                ins=[kv_loc.opt()],
                outs=[kv_ag.opt()],
            ).then_inc(C["c_ag"].h, 1))
            C["c_ag"].n += 1
            if debug >= 2:
                wt("gp", C["c_ag"], C["c_ag"].n)
                em("gp", lambda g: g.dma_start(out=kvl_dump, in_=kv_loc).then_inc(C["g0"].h, 16))
                C["g0"].n += 16
                em("gp", lambda g: g.dma_start(out=kvag_dump, in_=kv_ag).then_inc(C["g0"].h, 16))
                C["g0"].n += 16
                wt("gp", C["g0"], C["g0"].n)

        def p5():
            wt("sy", C["c_ag"], C["c_ag"].n)
            wt("sy", C["q_mm"], C["q_mm"].n)
            wt("ve", C["f_dpe"], C["f_dpe"].n)
            pv_hist = []
            omm_hist = []
            for j in range(NTT):
                wt("gp", C["a_res"], C["a_res"].n)
                wt("gp", C["f_xs"], C["f_xs"].n)
                em("gp", lambda e, i=x1_d[:, :, j * TT:(j + 1) * TT]:
                   e.dma_start(out=xres[:], in_=i).then_inc(C["f_x"].h, 16))
                C["f_x"].n += 16
                x_mark = C["f_x"].n
                ext = 8 * j + 8
                nkc = (j + 1) * 512
                for h in range(H):
                    hsl = h % 2
                    gh = j * H + h
                    if gh >= 2:
                        wt("sy", C["a_pv"], pv_hist[gh - 2])
                    em("sy", lambda e, hsl=hsl, h=h, j=j:
                       e.dma_start(out=qsb[hsl][:], in_=qf_d[h, :, j * TT:(j + 1) * TT])
                       .then_inc(C["a_ld"].h, 16))
                    C["a_ld"].n += 16
                    for r in range(2):
                        ksrc = kv_ag[r * 2 * D + h * 128: r * 2 * D + (h + 1) * 128, 0:nkc]
                        em("sy", lambda e, i=ksrc, hsl=hsl, r=r, nkc=nkc:
                           e.dma_start(out=ksb[hsl][:, r * 2048:r * 2048 + nkc], in_=i)
                           .then_inc(C["a_ld"].h, 16))
                        C["a_ld"].n += 16
                        vsrc = kv_ag[r * 2 * D + D: r * 2 * D + D + nkc, h * 128:(h + 1) * 128] \
                            .rearrange("(a p) e -> p a e", p=128)
                        em("sy", lambda e, i=vsrc, hsl=hsl, r=r, nkc=nkc:
                           e.dma_start(out=vsb[hsl][:, r * 2048:r * 2048 + nkc], in_=i)
                           .then_inc(C["a_ld"].h, 16))
                        C["a_ld"].n += 16
                    msk_at = {}

                    def emit_pv(i, ext=ext, hsl=hsl):
                        masked = i >= ext - 8
                        g = C["a_pv"].n
                        gk = i // 4
                        r, lb, q4 = gk % 2, gk // 2, i % 4
                        vcol = r * 2048 + (lb * 4 + q4) * 128
                        if masked:
                            wt("te", C["a_msk"], msk_at[i])
                            pt = ptm[g % 4]
                        else:
                            wt("te", C["a_exp"], g + 1)
                            pt = ptr_[g % 4]
                        if i == 0:
                            wt("te", C["a_z"], C["a_z"].n)
                            wt("te", C["a_rbe"], C["a_rbe"].n)
                        em("te", lambda t, hsl=hsl, vcol=vcol, pt=pt, i=i, ext=ext: t.matmul(
                            bank[4][:], vsb[hsl][:, vcol:vcol + 128], pt[:],
                            start=(i == 0), stop=(i == ext - 1)))
                        em("te", lambda t, pt=pt, i=i, ext=ext: t.matmul(
                            bank[5][0:1, :], ones_col[:], pt[:],
                            start=(i == 0), stop=(i == ext - 1)).then_inc(C["a_pv"].h, 1))
                        C["a_pv"].n += 1

                    wt("te", C["a_ld"], C["a_ld"].n)
                    for i in range(ext):
                        gk = i // 4
                        r, lb, q4 = gk % 2, gk // 2, i % 4
                        kcol = r * 2048 + lb * 512 + q4 * 128
                        g = C["a_s"].n
                        if g >= 4:
                            wt("te", C["a_exp"], g - 3)
                        em("te", lambda t, hsl=hsl, kcol=kcol, g=g: t.matmul(
                            bank[g % 4][:], ksb[hsl][:, kcol:kcol + 128], qsb[hsl][:],
                            start=True, stop=True).then_inc(C["a_s"].h, 1))
                        C["a_s"].n += 1
                        wt("sc", C["a_s"], C["a_s"].n)
                        if g >= 4:
                            wt("sc", C["a_pv"], g - 3)
                        em("sc", lambda s, g=g: s.activation(
                            ptr_[g % 4][:], bank[g % 4][:], AF.Exp, bias=c_m25[:], scale=SCALE_QK)
                            .then_inc(C["a_exp"].h, 1))
                        C["a_exp"].n += 1
                        if i >= ext - 8:
                            mi = i - (ext - 8)
                            wt("ve", C["a_exp"], C["a_exp"].n)
                            if g >= 4:
                                wt("ve", C["a_pv"], g - 3)
                            em("ve", lambda v, g=g, mi=mi: v.tensor_tensor(
                                ptm[g % 4][:], ptr_[g % 4][:], masks[:, mi * TT:(mi + 1) * TT],
                                op=ALU.mult).then_inc(C["a_msk"].h, 1))
                            C["a_msk"].n += 1
                            msk_at[i] = C["a_msk"].n
                        if i >= 1:
                            emit_pv(i - 1)
                    emit_pv(ext - 1)
                    pv_hist.append(C["a_pv"].n)
                    wt("ve", C["a_pv"], C["a_pv"].n)
                    em("ve", lambda v: v.tensor_scalar(rows[:, 0:512], bank[5][0:1, :], 1.0, None, op0=ALU.mult))
                    em("ve", lambda v: v.reciprocal(rows[:, 512:1024], rows[:, 0:512]))
                    em("ve", lambda v: v.tensor_copy(rowsb[:, 0:512], rows[:, 512:1024])
                       .then_inc(C["a_den"].h, 1))
                    C["a_den"].n += 1
                    wt("te", C["a_den"], C["a_den"].n)
                    em("te", lambda t: t.matmul(bank[5][:], ones1[:], rowsb[:, 0:512], start=True, stop=True)
                       .then_inc(C["a_rb"].h, 1))
                    C["a_rb"].n += 1
                    wt("ve", C["a_rb"], C["a_rb"].n)
                    em("ve", lambda v: v.tensor_copy(rb2[:], bank[5][:]).then_inc(C["a_rbe"].h, 1))
                    C["a_rbe"].n += 1
                    em("ve", lambda v, h=h: v.tensor_tensor(
                        zall[:, h * TT:(h + 1) * TT], bank[4][:], rb2[:], op=ALU.mult)
                        .then_inc(C["a_z"].h, 1))
                    C["a_z"].n += 1
                for c in range(NCH):
                    csl = c % 2
                    gc = j * NCH + c
                    if gc >= 2:
                        wt("gp", C["a_omm"], omm_hist[gc - 2])
                    em("gp", lambda e, i=wo_in[:, :, c * 128:(c + 1) * 128], csl=csl: e.dma_start(
                        out=wob[csl][:], in_=i).then_inc(C["a_wo"].h, 16))
                    C["a_wo"].n += 16
                    wt("te", C["a_wo"], C["a_wo"].n)
                    wt("te", C["a_z"], C["a_z"].n)
                    if gc >= 2:
                        wt("te", C["a_res"], C["a_res"].n - 1)
                    for h in range(H):
                        if h == H - 1:
                            em("te", lambda t, h=h, csl=csl: t.matmul(
                                bank[6 + csl][:], wob[csl][:, h * 128:(h + 1) * 128],
                                zall[:, h * TT:(h + 1) * TT], start=False, stop=True)
                                .then_inc(C["a_omm"].h, 1))
                        else:
                            em("te", lambda t, h=h, csl=csl: t.matmul(
                                bank[6 + csl][:], wob[csl][:, h * 128:(h + 1) * 128],
                                zall[:, h * TT:(h + 1) * TT], start=(h == 0), stop=False))
                    C["a_omm"].n += 1
                    omm_hist.append(C["a_omm"].n)
                    wt("ve", C["a_omm"], C["a_omm"].n)
                    wt("ve", C["f_x"], x_mark)
                    xs = xres[:, c * TT:(c + 1) * TT]
                    em("ve", lambda v, xs=xs, csl=csl: v.tensor_tensor(
                        xs, xs, bank[6 + csl][:], op=ALU.add).then_inc(C["a_res"].h, 1))
                    C["a_res"].n += 1
                wt("gp", C["a_res"], C["a_res"].n)
                em("gp", lambda e, o=x2_d[:, :, j * TT:(j + 1) * TT]:
                   e.dma_start(out=o, in_=xres[:]).then_inc(C["f_xs"].h, 16))
                C["f_xs"].n += 16

        for _nm in ("d_st0", "d_st1", "d_st2", "d_st3"):
            wt("sy", C[_nm], C[_nm].n)
        ffn_pass(x_in, x1_d)
        p3()
        p4()
        p5()
        wt("sy", C["a_pv"], C["a_pv"].n)
        wt("sy", C["a_omm"], C["a_omm"].n)
        wt("ve", C["a_omm"], C["a_omm"].n)
        ffn_pass(x2_d, xout)
        wt("sy", C["f_xs"], C["f_xs"].n)

        with nc.Block() as block:
            @block.sync
            def _(e):
                for fn in Q["sy"]:
                    fn(e)

            @block.gpsimd
            def _(e):
                for fn in Q["gp"]:
                    fn(e)

            @block.vector
            def _(e):
                for fn in Q["ve"]:
                    fn(e)

            @block.scalar
            def _(e):
                for fn in Q["sc"]:
                    fn(e)

            @block.tensor
            def _(e):
                for fn in Q["te"]:
                    fn(e)

    return nc


def _host_prepare(inputs):
    x = np.asarray(inputs["L0_attn_x"], dtype=np.float32)
    WQ = np.asarray(inputs["L0_attn_W_Q"], dtype=np.float32)
    WK = np.asarray(inputs["L0_attn_W_K"], dtype=np.float32)
    WV = np.asarray(inputs["L0_attn_W_V"], dtype=np.float32)
    WO = np.asarray(inputs["L0_attn_W_O"], dtype=np.float32)

    def unpack(Wp):
        b = np.asarray(Wp).astype(np.uint8)
        nib = np.empty((b.shape[0], b.shape[1] * 2), np.uint8)
        nib[:, 0::2] = b & 15
        nib[:, 1::2] = (b >> 4) & 15
        return nib

    nibg = unpack(inputs["L0_ffn_gate_W_packed"])
    nibu = unpack(inputs["L0_ffn_up_W_packed"])
    nibd = unpack(inputs["L0_ffn_down_W_packed"])

    def pack_gu(nib):
        A = nib.reshape(NGR, 4, 512, NCH, 128).astype(np.uint16)
        u = A[:, 0] | (A[:, 1] << 4) | (A[:, 2] << 8) | (A[:, 3] << 12)
        return np.ascontiguousarray(u.transpose(2, 0, 3, 1))

    u16g_a = pack_gu(nibg)
    u16u_a = pack_gu(nibu)
    Y = nibd.reshape(4, 512, 128, 128).astype(np.uint16)
    u16d_a = np.ascontiguousarray(
        (Y[0] | (Y[1] << 4) | (Y[2] << 8) | (Y[3] << 12)).transpose(1, 2, 0))

    sgT = np.ascontiguousarray(np.asarray(inputs["L0_ffn_gate_scales"], dtype=np.float32).T)
    suT = np.ascontiguousarray(np.asarray(inputs["L0_ffn_up_scales"], dtype=np.float32).T)
    sdT = np.ascontiguousarray(np.asarray(inputs["L0_ffn_down_scales"], dtype=np.float32).T)

    def _whost(W):
        return np.ascontiguousarray(W.reshape(H, E, NCH, 128).transpose(0, 3, 2, 1))

    wq = _whost(WQ)
    wk = _whost(WK)
    wv = _whost(WV)
    wo = np.ascontiguousarray(WO.transpose(2, 0, 1))

    in_maps = []
    core_rows = []
    for c in range(8):
        b, p = c // 2, c % 2
        gids = [2 * jj + p for jj in range(NTT)]
        rows_idx = np.concatenate([np.arange(g * TT, (g + 1) * TT) for g in gids])
        core_rows.append((b, rows_idx))
        xc = np.ascontiguousarray(x[b][rows_idx].T.reshape(NCH, 128, TC).transpose(1, 0, 2))
        kv = (np.arange(128, dtype=np.float32)[:, None]
              + 128.0 * np.arange(8, dtype=np.float32)[None, :] - 512.0 * p)
        in_maps.append({
            "x": xc, "wq": wq, "wk": wk, "wv": wv, "wo": wo,
            "u16g": u16g_a, "u16u": u16u_a, "u16d": u16d_a,
            "sgT": sgT, "suT": suT, "sdT": sdT,
            "kvec": np.ascontiguousarray(kv),
        })
    return in_maps, core_rows


def kernel(**inputs):
    in_maps, core_rows = _host_prepare(inputs)
    nc = build_program()
    res = run_bass_kernel_spmd(nc, in_maps, core_ids=list(range(8)))
    out = np.empty((B, S, D), dtype=np.float32)
    for c in range(8):
        b, rows_idx = core_rows[c]
        xo = np.asarray(res.results[c]["xout"], dtype=np.float32)
        out[b][rows_idx] = xo.transpose(1, 0, 2).reshape(D, TC).T
    return out


# version 10
# speedup vs baseline: 1.0316x; 1.0316x over previous
import numpy as np
import ml_dtypes
from contextlib import ExitStack

import concourse.bass as bass
import concourse.mybir as mybir
from concourse.bass_utils import run_bass_kernel_spmd

F32 = mybir.dt.float32
BF16 = mybir.dt.bfloat16
U16 = mybir.dt.uint16
U32 = mybir.dt.uint32
U8 = mybir.dt.uint8
ALU = mybir.AluOpType
AF = mybir.ActivationFunctionType

B, S, D, H, E, NM = 4, 4096, 2048, 16, 128, 16384
TC = 2048
TT = 512
NTT = TC // TT
NCH = D // 128
NTN = NM // 128
NGR = 8
NPG = NTN // NGR
SCALE_QK = 1.0 / float(np.sqrt(128.0))
EXP_SHIFT = -25.0
K_G1 = 0.7978845608028654
K_G2 = K_G1 * 0.044715

ISZ = {F32: 4, BF16: 2, U16: 2, U8: 1}


class Ctr:
    def __init__(self, h):
        self.h = h
        self.n = 0


def _carve(arena, off, p, cols, dt):
    nb = cols * ISZ[dt]
    assert off % 64 == 0 and off + nb <= 134 * 1024, (off, nb)
    return arena[0:p, off:off + nb].bitcast(dt)


def build_program(debug=0):
    nc = bass.Bass("TRN2", target_bir_lowering=False)

    x_in = nc.dram_tensor("x", [128, NCH, TC], F32, kind="ExternalInput").ap()
    wq_in = nc.dram_tensor("wq", [H, 128, NCH, E], F32, kind="ExternalInput").ap()
    wk_in = nc.dram_tensor("wk", [H, 128, NCH, E], F32, kind="ExternalInput").ap()
    wv_in = nc.dram_tensor("wv", [H, 128, NCH, E], F32, kind="ExternalInput").ap()
    wo_in = nc.dram_tensor("wo", [128, H, D], F32, kind="ExternalInput").ap()
    u16g = nc.dram_tensor("u16g", [NCH, NGR, 128, 512], U16, kind="ExternalInput").ap()
    u16u = nc.dram_tensor("u16u", [NCH, NGR, 128, 512], U16, kind="ExternalInput").ap()
    u16d = nc.dram_tensor("u16d", [128, 128, 512], U16, kind="ExternalInput").ap()
    sgT = nc.dram_tensor("sgT", [64, NM], F32, kind="ExternalInput").ap()
    suT = nc.dram_tensor("suT", [64, NM], F32, kind="ExternalInput").ap()
    sdT = nc.dram_tensor("sdT", [512, D], F32, kind="ExternalInput").ap()
    kvec_in = nc.dram_tensor("kvec", [128, 8], F32, kind="ExternalInput").ap()

    xout = nc.dram_tensor("xout", [128, NCH, TC], F32, kind="ExternalOutput").ap()

    k1 = "Internal" if debug < 1 else "ExternalOutput"
    k2 = "Internal" if debug < 2 else "ExternalOutput"
    wg_d = nc.dram_tensor("wg_d", [NTN, 128, NCH, 128], BF16, kind=k2).ap()
    wu_d = nc.dram_tensor("wu_d", [NTN, 128, NCH, 128], BF16, kind=k2).ap()
    wd_d = nc.dram_tensor("wd_d", [NGR, 4, 128, NPG, 512], BF16, kind=k2).ap()
    x1_d = nc.dram_tensor("x1_d", [128, NCH, TC], F32, kind=k1).ap()
    x2_d = nc.dram_tensor("x2_d", [128, NCH, TC], F32, kind=k1).ap()
    qf_d = nc.dram_tensor("qf_d", [H, E, TC], BF16, kind=k2).ap()
    kv_loc = nc.dram_tensor("kv_loc", [2 * D, TC], BF16, kind="Internal").ap()
    kv_ag = nc.dram_tensor("kv_ag", [4 * D, TC], BF16, kind="Internal").ap()
    if debug >= 2:
        kvl_dump = nc.dram_tensor("kvl_dump", [2 * D, TC], BF16, kind="ExternalOutput").ap()
        kvag_dump = nc.dram_tensor("kvag_dump", [4 * D, TC], BF16, kind="ExternalOutput").ap()

    with ExitStack() as ctx:
        EN = ctx.enter_context

        xres = EN(nc.sbuf_tensor("xres", [128, NCH * TT], F32))
        xln = EN(nc.sbuf_tensor("xln", [128, NCH * TT], BF16))
        masks = EN(nc.sbuf_tensor("masks", [128, 8 * TT], BF16))
        iq = EN(nc.sbuf_tensor("iq", [128, TT], F32))
        kvec = EN(nc.sbuf_tensor("kvec_sb", [128, 8], F32))
        pmi = EN(nc.sbuf_tensor("pmi", [128, 128], F32))
        identf = EN(nc.sbuf_tensor("identf", [128, 128], F32))
        ones_col = EN(nc.sbuf_tensor("ones_col", [128, 1], BF16))
        ones1 = EN(nc.sbuf_tensor("ones1", [1, 128], BF16))
        c_m25 = EN(nc.sbuf_tensor("c_m25", [128, 1], F32))
        c_eps = EN(nc.sbuf_tensor("c_eps", [1, 1], F32))
        rows = EN(nc.sbuf_tensor("rows", [1, 1536], F32))
        rowsb = EN(nc.sbuf_tensor("rowsb", [1, 1024], BF16))
        arena = EN(nc.sbuf_tensor("arena", [128, 134 * 1024], U8))

        u16st = [_carve(arena, 0 + i * 1024, 128, 512, U16) for i in range(4)]
        sclf = [_carve(arena, 4096 + i * 8192, 128, 2048, F32) for i in range(4)]
        sclb = _carve(arena, 36864, 128, 2048, BF16)
        xfull = _carve(arena, 40960, 128, 2048, U16)
        dsg = _carve(arena, 45056, 128, 2048, U16)
        dt1 = _carve(arena, 49152, 128, 2048, BF16)
        decb = [_carve(arena, 53248 + i * 4096, 128, 2048, BF16) for i in range(4)]

        wgb = [_carve(arena, 0 + i * 4096, 128, 2048, BF16) for i in range(2)]
        wub = [_carve(arena, 8192 + i * 4096, 128, 2048, BF16) for i in range(2)]
        wdb = [_carve(arena, 16384 + i * 16384, 128, 8192, BF16) for i in range(2)]
        inter = _carve(arena, 49152, 128, NPG * TT, BF16)
        dacc = _carve(arena, 65536, 128, 4 * D, F32)
        xball = _carve(arena, 98304, 128, NCH * TT, BF16)
        sqs = [_carve(arena, 114688 + i * 1024, 128, 512, BF16) for i in range(2)]
        gsq = [_carve(arena, 116736 + i * 2048, 128, 512, F32) for i in range(2)]
        gav = _carve(arena, 120832, 128, 512, F32)
        ginn = [_carve(arena, 122880 + i * 2048, 128, 512, F32) for i in range(2)]
        gtha = [_carve(arena, 126976 + i * 2048, 128, 512, F32) for i in range(2)]
        uh = [_carve(arena, 131072 + i * 2048, 128, 512, F32) for i in range(2)]
        mb_sb = _carve(arena, 135168, 128, 512, BF16)
        rb_sb = _carve(arena, 136192, 128, 512, BF16)

        wqb = [_carve(arena, 0 + i * 4096, 128, 2048, BF16) for i in range(2)]
        wkb = [_carve(arena, 8192 + i * 4096, 128, 2048, BF16) for i in range(2)]
        wvb = [_carve(arena, 16384 + i * 4096, 128, 2048, BF16) for i in range(2)]
        qkt = [_carve(arena, 24576 + i * 1024, 128, 512, BF16) for i in range(6)]

        qsb = [_carve(arena, 0 + i * 1024, 128, 512, BF16) for i in range(2)]
        ksb = [_carve(arena, 2048 + i * 8192, 128, 4096, BF16) for i in range(2)]
        vsb = [_carve(arena, 18432 + i * 8192, 128, 4096, BF16) for i in range(2)]
        ptr_ = [_carve(arena, 64512 + i * 1024, 128, 512, BF16) for i in range(4)]
        ptm = [_carve(arena, 68608 + i * 1024, 128, 512, BF16) for i in range(4)]
        zall = _carve(arena, 38912, 128, H * TT, BF16)
        wob = [_carve(arena, 55296 + i * 4096, 128, 2048, BF16) for i in range(2)]
        rb2 = _carve(arena, 63488, 128, 512, BF16)

        bank = [EN(nc.psum_tensor(f"bk{i}", [128, 512], F32)) for i in range(8)]

        names = ["g0", "g1", "v0", "d_u0", "d_u1", "d_u2", "d_u3", "d_s0", "d_s1", "d_s2", "d_s3", "d_st0", "d_st1", "d_st2", "d_st3", "d_v",
                 "f_x", "f_lnv", "f_lnmm", "f_st1", "f_st2", "f_st3",
                 "f_bc", "f_ap", "f_w", "f_gmm", "f_sqa", "f_inn",
                 "f_tha", "f_epi", "f_dw", "f_dpe", "f_dadd", "f_tp",
                 "f_res", "f_xs",
                 "q_w", "q_mm", "q_ev", "q_st0", "q_st1", "c_ag",
                 "a_ld", "a_s", "a_exp", "a_msk", "a_pv", "a_den",
                 "a_rb", "a_rbe", "a_z", "a_wo", "a_omm", "a_res"]
        C = {nm: Ctr(EN(nc.semaphore(nm))) for nm in names}

        Q = {"sy": [], "gp": [], "ve": [], "sc": [], "te": []}

        def em(qn, fn):
            Q[qn].append(fn)

        def wt(qn, ctr, val):
            if val > 0:
                em(qn, lambda e, s=ctr.h, v=val: e.wait_ge(s, v))

        em("gp", lambda g: g.dma_start(out=kvec[:], in_=kvec_in).then_inc(C["g1"].h, 16))
        C["g1"].n += 16
        em("gp", lambda g: g.iota(iq[:], pattern=[[1, TT]], base=0, channel_multiplier=0,
                                  allow_small_or_imprecise_dtypes=True).then_inc(C["g0"].h, 1))
        C["g0"].n += 1
        em("gp", lambda g: g.iota(pmi[:], pattern=[[-1, 128]], base=0, channel_multiplier=1,
                                  allow_small_or_imprecise_dtypes=True).then_inc(C["g0"].h, 1))
        C["g0"].n += 1
        wt("ve", C["g0"], C["g0"].n)
        wt("ve", C["g1"], C["g1"].n)
        for i in range(8):
            em("ve", lambda v, i=i: v.tensor_scalar(
                masks[:, i * TT:(i + 1) * TT], iq[:], kvec[:, i:i + 1], None, op0=ALU.is_ge))
        em("ve", lambda v: v.tensor_scalar(identf[:], pmi[:], 0.0, None, op0=ALU.is_equal))
        em("ve", lambda v: v.memset(ones_col[:], 1.0))
        em("ve", lambda v: v.memset(ones1[:], 1.0))
        em("ve", lambda v: v.memset(c_m25[:], EXP_SHIFT))
        em("ve", lambda v: v.memset(c_eps[:], 1e-5).then_inc(C["v0"].h, 1))
        C["v0"].n += 1
        wt("sc", C["v0"], 1)
        wt("te", C["v0"], 1)

        blocks = []
        for G in range(NGR):
            blocks += [("g", c, G) for c in range(NCH)]
            blocks += [("u", c, G) for c in range(NCH)]
            blocks += [("d", G * NPG + j, G) for j in range(NPG)]
        NBLK = len(blocks)

        for k, (m, a, G) in enumerate(blocks):
            m_idx = {"g": 0, "u": 1, "d": 2}[m]
            if k >= 4:
                wt("sy", C["d_v"], k - 3)
            src = (u16g[a, G] if m == "g" else u16u[a, G] if m == "u" else u16d[a])
            em("sy", lambda e, o=u16st[k % 4], i=src, k=k:
               e.dma_start(out=o, in_=i).then_inc(C[f"d_u{k % 4}"].h, 16))
            C[f"d_u{k % 4}"].n += 16
            c_in = a % NPG if m == "d" else a
            if m == "g":
                ssrc = sgT[4 * c_in:4 * c_in + 4, G * 2048:(G + 1) * 2048]
            elif m == "u":
                ssrc = suT[4 * c_in:4 * c_in + 4, G * 2048:(G + 1) * 2048]
            else:
                ssrc = sdT[4 * a:4 * a + 4, :]
            if k >= 4:
                wt("gp", C["d_v"], k - 3)
            em("gp", lambda e, o=sclf[k % 4], i=ssrc, k=k:
               e.dma_start(out=o, in_=i.unsqueeze(1).broadcast_to([4, 32, 2048])).then_inc(C[f"d_s{k % 4}"].h, 16))
            C[f"d_s{k % 4}"].n += 16
            wt("ve", C[f"d_u{k % 4}"], 16 * (k // 4 + 1))
            u32s = u16st[k % 4].bitcast(U32)
            xf32 = xfull.bitcast(U32)
            sg32v = dsg.bitcast(U32)
            em("ve", lambda v, u=u32s: v.tensor_scalar(xf32[:, 0:256], u, 6, 0x01C001C0,
                                                       op0=ALU.logical_shift_left, op1=ALU.bitwise_and))
            em("ve", lambda v, u=u32s: v.tensor_scalar(xf32[:, 256:512], u, 2, 0x01C001C0,
                                                       op0=ALU.logical_shift_left, op1=ALU.bitwise_and))
            em("ve", lambda v, u=u32s: v.tensor_scalar(xf32[:, 512:768], u, 2, 0x01C001C0,
                                                       op0=ALU.logical_shift_right, op1=ALU.bitwise_and))
            em("ve", lambda v, u=u32s: v.tensor_scalar(xf32[:, 768:1024], u, 6, 0x01C001C0,
                                                       op0=ALU.logical_shift_right, op1=ALU.bitwise_and))
            em("ve", lambda v, u=u32s: v.tensor_scalar(sg32v[:, 0:256], u, 12, 0x80008000,
                                                       op0=ALU.logical_shift_left, op1=ALU.bitwise_and))
            em("ve", lambda v, u=u32s: v.tensor_scalar(sg32v[:, 256:512], u, 8, 0x80008000,
                                                       op0=ALU.logical_shift_left, op1=ALU.bitwise_and))
            em("ve", lambda v, u=u32s: v.tensor_scalar(sg32v[:, 512:768], u, 4, 0x80008000,
                                                       op0=ALU.logical_shift_left, op1=ALU.bitwise_and))
            em("ve", lambda v, u=u32s: v.tensor_scalar(sg32v[:, 768:1024], u, 0, 0x80008000,
                                                       op0=ALU.logical_shift_left, op1=ALU.bitwise_and))
            em("ve", lambda v: v.tensor_scalar(xfull[:], xfull[:], 0x3F00, None, op0=ALU.add))
            em("ve", lambda v: v.tensor_scalar(dt1[:], xfull[:].bitcast(BF16), 2.0, -1.0,
                                               op0=ALU.mult, op1=ALU.add))
            em("ve", lambda v: v.tensor_tensor(dt1[:], xfull[:].bitcast(BF16), dt1[:], op=ALU.min))
            wt("ve", C[f"d_s{k % 4}"], 16 * (k // 4 + 1))
            em("ve", lambda v, s=sclf[k % 4]: v.tensor_copy(sclb[:], s))
            em("ve", lambda v: v.tensor_tensor(
                dsg[:].bitcast(U32), sclb[:].bitcast(U32), dsg[:].bitcast(U32), op=ALU.bitwise_or))
            if k >= 4:
                wt("ve", C[f"d_st{k % 4}"], 16 * (k // 4))
            em("ve", lambda v, o=decb[k % 4]: v.tensor_tensor(
                o, dt1[:], dsg[:].bitcast(BF16), op=ALU.mult).then_inc(C["d_v"].h, 1))
            C["d_v"].n += 1
            wt("sc", C["d_v"], k + 1)
            if m in ("g", "u"):
                dst = (wg_d if m == "g" else wu_d)[G * NPG:(G + 1) * NPG, :, a, :] \
                    .rearrange("a p b -> p a b")
            else:
                dst = wd_d[a // NPG, :, :, a % NPG, :].rearrange("a p b -> p a b")
            em("sc", lambda e, o=dst, i=decb[k % 4], k=k:
               e.dma_start(out=o, in_=i).then_inc(C[f"d_st{k % 4}"].h, 16))
            C[f"d_st{k % 4}"].n += 16

        def ln_tile():
            wt("ve", C["f_x"], C["f_x"].n)
            wt("ve", C["f_lnmm"], C["f_lnmm"].n)
            for c in range(NCH):
                em("ve", lambda v, c=c: v.tensor_copy(
                    xball[:, c * TT:(c + 1) * TT], xres[:, c * TT:(c + 1) * TT]))
                if C["f_lnmm"].n >= 1:
                    wt("ve", C["f_lnmm"], C["f_lnmm"].n - 1)
                em("ve", lambda v, c=c: v.tensor_tensor(
                    sqs[c % 2], xball[:, c * TT:(c + 1) * TT],
                    xball[:, c * TT:(c + 1) * TT], op=ALU.mult).then_inc(C["f_lnv"].h, 1))
                C["f_lnv"].n += 1
                wt("te", C["f_lnv"], C["f_lnv"].n)
                if c == 0:
                    wt("te", C["f_res"], C["f_res"].n)
                    wt("te", C["a_res"], C["a_res"].n)
                em("te", lambda t, c=c: t.matmul(
                    bank[6][0:1, :], ones_col[:], xball[:, c * TT:(c + 1) * TT],
                    start=(c == 0), stop=(c == NCH - 1)))
                em("te", lambda t, c=c: t.matmul(
                    bank[7][0:1, :], ones_col[:], sqs[c % 2],
                    start=(c == 0), stop=(c == NCH - 1)).then_inc(C["f_lnmm"].h, 1))
                C["f_lnmm"].n += 1
            wt("ve", C["f_lnmm"], C["f_lnmm"].n)
            em("ve", lambda v: v.tensor_scalar(rows[:, 0:512], bank[6][0:1, :], 1.0 / 2048.0, None, op0=ALU.mult))
            em("ve", lambda v: v.tensor_scalar(rows[:, 512:1024], bank[7][0:1, :], 1.0 / 2048.0, None, op0=ALU.mult))
            em("ve", lambda v: v.tensor_tensor(rows[:, 1024:1536], rows[:, 0:512], rows[:, 0:512], op=ALU.mult))
            em("ve", lambda v: v.tensor_tensor(
                rows[:, 512:1024], rows[:, 512:1024], rows[:, 1024:1536], op=ALU.subtract)
                .then_inc(C["f_st1"].h, 1))
            C["f_st1"].n += 1
            wt("sc", C["f_st1"], C["f_st1"].n)
            em("sc", lambda s: s.activation(rows[:, 1024:1536], rows[:, 512:1024], AF.Sqrt,
                                            bias=c_eps[:], scale=1.0).then_inc(C["f_st2"].h, 1))
            C["f_st2"].n += 1
            wt("ve", C["f_st2"], C["f_st2"].n)
            em("ve", lambda v: v.reciprocal(rows[:, 512:1024], rows[:, 1024:1536]))
            em("ve", lambda v: v.tensor_copy(rowsb[:, 0:512], rows[:, 0:512]))
            em("ve", lambda v: v.tensor_copy(rowsb[:, 512:1024], rows[:, 512:1024]).then_inc(C["f_st3"].h, 1))
            C["f_st3"].n += 1
            wt("te", C["f_st3"], C["f_st3"].n)
            em("te", lambda t: t.matmul(bank[6][:], ones1[:], rowsb[:, 0:512], start=True, stop=True))
            em("te", lambda t: t.matmul(bank[7][:], ones1[:], rowsb[:, 512:1024], start=True, stop=True)
               .then_inc(C["f_bc"].h, 1))
            C["f_bc"].n += 1
            wt("ve", C["f_bc"], C["f_bc"].n)
            em("ve", lambda v: v.tensor_copy(mb_sb[:], bank[6][:]))
            em("ve", lambda v: v.tensor_copy(rb_sb[:], bank[7][:]))
            wt("ve", C["f_gmm"], C["f_gmm"].n)
            wt("ve", C["q_mm"], C["q_mm"].n)
            for c in range(NCH):
                em("ve", lambda v, c=c: v.tensor_tensor(
                    xball[:, c * TT:(c + 1) * TT], xball[:, c * TT:(c + 1) * TT], mb_sb[:], op=ALU.subtract))
                em("ve", lambda v, c=c: v.tensor_tensor(
                    xln[:, c * TT:(c + 1) * TT], xball[:, c * TT:(c + 1) * TT], rb_sb[:], op=ALU.mult)
                    .then_inc(C["f_ap"].h, 1))
                C["f_ap"].n += 1

        def ffn_pass(src_ap, dst_ap):
            for ti in range(NTT):
                wt("gp", C["f_xs"], C["f_xs"].n)
                em("gp", lambda e, i=src_ap[:, :, ti * TT:(ti + 1) * TT]:
                   e.dma_start(out=xres[:], in_=i).then_inc(C["f_x"].h, 16))
                C["f_x"].n += 16
                ln_tile()
                for g in range(NGR):
                    for j in range(NPG):
                        jn = g * NPG + j
                        sl = jn % 2
                        if C["f_gmm"].n >= 1:
                            wt("sy", C["f_gmm"], C["f_gmm"].n - 1)
                        em("sy", lambda e, i=wg_d[jn], o=wgb[sl]: e.dma_start(
                            out=o, in_=i).then_inc(C["f_w"].h, 16))
                        C["f_w"].n += 16
                        em("sy", lambda e, i=wu_d[jn], o=wub[sl]: e.dma_start(
                            out=o, in_=i).then_inc(C["f_w"].h, 16))
                        C["f_w"].n += 16
                        wt("te", C["f_w"], C["f_w"].n)
                        wt("te", C["f_ap"], C["f_ap"].n)
                        if C["f_epi"].n >= 1:
                            wt("te", C["f_epi"], C["f_epi"].n - 1)
                        for c in range(NCH):
                            em("te", lambda t, c=c, sl=sl: t.matmul(
                                bank[sl][:], wgb[sl][:, c * 128:(c + 1) * 128],
                                xln[:, c * TT:(c + 1) * TT], start=(c == 0), stop=(c == NCH - 1)))
                        for c in range(NCH):
                            if c == NCH - 1:
                                em("te", lambda t, c=c, sl=sl: t.matmul(
                                    bank[2 + sl][:], wub[sl][:, c * 128:(c + 1) * 128],
                                    xln[:, c * TT:(c + 1) * TT], start=False, stop=True)
                                    .then_inc(C["f_gmm"].h, 1))
                            else:
                                em("te", lambda t, c=c, sl=sl: t.matmul(
                                    bank[2 + sl][:], wub[sl][:, c * 128:(c + 1) * 128],
                                    xln[:, c * TT:(c + 1) * TT], start=(c == 0), stop=False))
                        C["f_gmm"].n += 1
                        wt("sc", C["f_gmm"], C["f_gmm"].n)
                        em("sc", lambda s, sl=sl: s.activation(gsq[sl], bank[sl][:], AF.Square))
                        em("sc", lambda s, sl=sl: s.activation(uh[sl], bank[2 + sl][:], AF.Copy, scale=0.5)
                           .then_inc(C["f_sqa"].h, 1))
                        C["f_sqa"].n += 1
                        wt("ve", C["f_sqa"], C["f_sqa"].n)
                        if g == 0 and j == 0:
                            wt("ve", C["f_dpe"], C["f_dpe"].n)
                            wt("ve", C["a_omm"], C["a_omm"].n)
                        if j == 0:
                            wt("ve", C["f_dpe"], C["f_dpe"].n)
                        em("ve", lambda v, sl=sl: v.tensor_scalar(gav[:], gsq[sl], K_G2, K_G1,
                                                                  op0=ALU.mult, op1=ALU.add))
                        em("ve", lambda v, sl=sl: v.tensor_tensor(ginn[sl], gav[:], bank[sl][:], op=ALU.mult)
                           .then_inc(C["f_inn"].h, 1))
                        C["f_inn"].n += 1
                        wt("sc", C["f_inn"], C["f_inn"].n)
                        em("sc", lambda s, sl=sl: s.activation(gtha[sl], ginn[sl], AF.Tanh)
                           .then_inc(C["f_tha"].h, 1))
                        C["f_tha"].n += 1
                        wt("ve", C["f_tha"], C["f_tha"].n)
                        em("ve", lambda v, sl=sl: v.tensor_tensor(gav[:], bank[sl][:], uh[sl], op=ALU.mult))
                        em("ve", lambda v, sl=sl, j=j: v.scalar_tensor_tensor(
                            inter[:, j * TT:(j + 1) * TT], gtha[sl], 1.0, gav[:],
                            op0=ALU.add, op1=ALU.mult).then_inc(C["f_epi"].h, 1))
                        C["f_epi"].n += 1
                    for dt in range(4):
                        dsl = dt % 2
                        if C["f_dpe"].n >= 4:
                            wt("sy", C["f_dpe"], C["f_dpe"].n - 4)
                        em("sy", lambda e, i=wd_d[g, dt], o=wdb[dsl]: e.dma_start(
                            out=o, in_=i).then_inc(C["f_dw"].h, 16))
                        C["f_dw"].n += 16
                        wt("te", C["f_dw"], C["f_dw"].n)
                        wt("te", C["f_epi"], C["f_epi"].n)
                        for tsub in range(4):
                            bsl = 4 + (tsub % 2)
                            if C["f_dadd"].n >= 1:
                                wt("te", C["f_dadd"], C["f_dadd"].n - 1)
                            for j in range(NPG):
                                if j == NPG - 1:
                                    em("te", lambda t, j=j, tsub=tsub, dsl=dsl, bsl=bsl: t.matmul(
                                        bank[bsl][:],
                                        inter[:, j * TT + tsub * 128: j * TT + tsub * 128 + 128],
                                        wdb[dsl][:, j * 512:(j + 1) * 512],
                                        start=False, stop=True).then_inc(C["f_dpe"].h, 1))
                                else:
                                    em("te", lambda t, j=j, tsub=tsub, dsl=dsl, bsl=bsl: t.matmul(
                                        bank[bsl][:],
                                        inter[:, j * TT + tsub * 128: j * TT + tsub * 128 + 128],
                                        wdb[dsl][:, j * 512:(j + 1) * 512],
                                        start=(j == 0), stop=False))
                            C["f_dpe"].n += 1
                            wt("ve", C["f_dpe"], C["f_dpe"].n)
                            if g == 0:
                                wt("ve", C["f_tp"], C["f_tp"].n)
                            dslice = dacc[:, tsub * D + dt * 512: tsub * D + dt * 512 + 512]
                            if g == 0:
                                em("ve", lambda v, o=dslice, b=bank[bsl]: v.tensor_copy(o, b[:])
                                   .then_inc(C["f_dadd"].h, 1))
                            else:
                                em("ve", lambda v, o=dslice, b=bank[bsl]: v.tensor_tensor(
                                    o, o, b[:], op=ALU.add).then_inc(C["f_dadd"].h, 1))
                            C["f_dadd"].n += 1
                wt("te", C["f_dadd"], C["f_dadd"].n)
                wt("te", C["f_ap"], C["f_ap"].n)
                for tsub in range(4):
                    for c in range(NCH):
                        tb_ = 6 + ((tsub * NCH + c) % 2)
                        if C["f_tp"].n >= 2:
                            wt("te", C["f_res"], C["f_tp"].n - 1)
                        em("te", lambda t, tsub=tsub, c=c, tb_=tb_: t.transpose(
                            bank[tb_][:, 0:128],
                            dacc[:, tsub * D + c * 128: tsub * D + c * 128 + 128], identf[:])
                            .then_inc(C["f_tp"].h, 1))
                        C["f_tp"].n += 1
                        wt("ve", C["f_tp"], C["f_tp"].n)
                        xslice = xres[:, c * TT + tsub * 128: c * TT + tsub * 128 + 128]
                        em("ve", lambda v, x=xslice, tb_=tb_: v.scalar_tensor_tensor(
                            x, bank[tb_][:, 0:128], 0.5, x, op0=ALU.mult, op1=ALU.add)
                            .then_inc(C["f_res"].h, 1))
                        C["f_res"].n += 1
                wt("gp", C["f_res"], C["f_res"].n)
                em("gp", lambda e, o=dst_ap[:, :, ti * TT:(ti + 1) * TT]:
                   e.dma_start(out=o, in_=xres[:]).then_inc(C["f_xs"].h, 16))
                C["f_xs"].n += 16

        def p3():
            wt("gp", C["f_gmm"], C["f_gmm"].n)
            wt("gp", C["f_dpe"], C["f_dpe"].n)
            wt("ve", C["f_dpe"], C["f_dpe"].n)
            qmm_hist = []
            for ti in range(NTT):
                wt("gp", C["f_xs"], C["f_xs"].n)
                wt("gp", C["f_ap"], C["f_ap"].n)
                em("gp", lambda e, i=x1_d[:, :, ti * TT:(ti + 1) * TT]:
                   e.dma_start(out=xres[:], in_=i).then_inc(C["f_x"].h, 16))
                C["f_x"].n += 16
                ln_tile()
                for h in range(H):
                    hsl = h % 2
                    gi = ti * H + h
                    if gi >= 2:
                        wt("gp", C["q_mm"], qmm_hist[gi - 2])
                    for (w_in, buf) in ((wq_in, wqb), (wk_in, wkb), (wv_in, wvb)):
                        em("gp", lambda e, i=w_in[h], o=buf[hsl]: e.dma_start(
                            out=o, in_=i).then_inc(C["q_w"].h, 16))
                        C["q_w"].n += 16
                    wt("te", C["q_w"], C["q_w"].n)
                    wt("te", C["f_ap"], C["f_ap"].n)
                    if gi >= 2:
                        wt("te", C["q_ev"], C["q_ev"].n - 1)
                    for c in range(NCH):
                        em("te", lambda t, c=c, hsl=hsl: t.matmul(
                            bank[3 * hsl][:], wqb[hsl][:, c * 128:(c + 1) * 128],
                            xln[:, c * TT:(c + 1) * TT], start=(c == 0), stop=(c == NCH - 1)))
                    for c in range(NCH):
                        em("te", lambda t, c=c, hsl=hsl: t.matmul(
                            bank[3 * hsl + 1][:], wkb[hsl][:, c * 128:(c + 1) * 128],
                            xln[:, c * TT:(c + 1) * TT], start=(c == 0), stop=(c == NCH - 1)))
                    for c in range(NCH):
                        for ks in range(4):
                            last = (c == NCH - 1 and ks == 3)
                            if last:
                                em("te", lambda t, c=c, ks=ks, hsl=hsl: t.matmul(
                                    bank[3 * hsl + 2][:, ks * 128:(ks + 1) * 128],
                                    xln[:, c * TT + ks * 128: c * TT + ks * 128 + 128],
                                    wvb[hsl][:, c * 128:(c + 1) * 128],
                                    start=False, stop=True).then_inc(C["q_mm"].h, 1))
                            else:
                                em("te", lambda t, c=c, ks=ks, hsl=hsl: t.matmul(
                                    bank[3 * hsl + 2][:, ks * 128:(ks + 1) * 128],
                                    xln[:, c * TT + ks * 128: c * TT + ks * 128 + 128],
                                    wvb[hsl][:, c * 128:(c + 1) * 128],
                                    start=(c == 0 and ks == 0), stop=False))
                    C["q_mm"].n += 1
                    qmm_hist.append(C["q_mm"].n)
                    wt("ve", C["q_mm"], C["q_mm"].n)
                    if gi >= 2:
                        wt("ve", C[f"q_st{gi % 2}"], 48 * (gi // 2))
                    em("ve", lambda v, hsl=hsl: v.tensor_copy(qkt[3 * hsl][:], bank[3 * hsl][:]))
                    em("ve", lambda v, hsl=hsl: v.tensor_copy(qkt[3 * hsl + 1][:], bank[3 * hsl + 1][:]))
                    em("ve", lambda v, hsl=hsl: v.tensor_copy(qkt[3 * hsl + 2][:], bank[3 * hsl + 2][:])
                       .then_inc(C["q_ev"].h, 1))
                    C["q_ev"].n += 1
                    wt("sy", C["q_ev"], C["q_ev"].n)
                    em("sy", lambda e, o=qf_d[h, :, ti * TT:(ti + 1) * TT], i=qkt[3 * hsl][:], gi=gi:
                       e.dma_start(out=o, in_=i).then_inc(C[f"q_st{gi % 2}"].h, 16))
                    C[f"q_st{gi % 2}"].n += 16
                    em("sy", lambda e, o=kv_loc[h * 128:(h + 1) * 128, ti * TT:(ti + 1) * TT],
                       i=qkt[3 * hsl + 1][:], gi=gi: e.dma_start(out=o, in_=i).then_inc(C[f"q_st{gi % 2}"].h, 16))
                    C[f"q_st{gi % 2}"].n += 16
                    vdst = kv_loc[D + ti * TT: D + (ti + 1) * TT, h * 128:(h + 1) * 128] \
                        .rearrange("(a p) e -> p a e", p=128)
                    em("sy", lambda e, o=vdst, i=qkt[3 * hsl + 2][:], gi=gi:
                       e.dma_start(out=o, in_=i).then_inc(C[f"q_st{gi % 2}"].h, 16))
                    C[f"q_st{gi % 2}"].n += 16

        def p4():
            wt("gp", C["q_st0"], C["q_st0"].n)
            wt("gp", C["q_st1"], C["q_st1"].n)
            em("gp", lambda g: g.collective_compute(
                "AllGather", ALU.bypass,
                replica_groups=[[0, 1], [2, 3], [4, 5], [6, 7]],
                ins=[kv_loc.opt()],
                outs=[kv_ag.opt()],
            ).then_inc(C["c_ag"].h, 1))
            C["c_ag"].n += 1
            if debug >= 2:
                wt("gp", C["c_ag"], C["c_ag"].n)
                em("gp", lambda g: g.dma_start(out=kvl_dump, in_=kv_loc).then_inc(C["g0"].h, 16))
                C["g0"].n += 16
                em("gp", lambda g: g.dma_start(out=kvag_dump, in_=kv_ag).then_inc(C["g0"].h, 16))
                C["g0"].n += 16
                wt("gp", C["g0"], C["g0"].n)

        def p5():
            wt("sy", C["c_ag"], C["c_ag"].n)
            wt("sy", C["q_mm"], C["q_mm"].n)
            wt("ve", C["f_dpe"], C["f_dpe"].n)
            pv_hist = []
            omm_hist = []
            for j in range(NTT):
                wt("gp", C["a_res"], C["a_res"].n)
                wt("gp", C["f_xs"], C["f_xs"].n)
                em("gp", lambda e, i=x1_d[:, :, j * TT:(j + 1) * TT]:
                   e.dma_start(out=xres[:], in_=i).then_inc(C["f_x"].h, 16))
                C["f_x"].n += 16
                x_mark = C["f_x"].n
                ext = 8 * j + 8
                nkc = (j + 1) * 512
                for h in range(H):
                    hsl = h % 2
                    gh = j * H + h
                    if gh >= 2:
                        wt("sy", C["a_pv"], pv_hist[gh - 2])
                    em("sy", lambda e, hsl=hsl, h=h, j=j:
                       e.dma_start(out=qsb[hsl][:], in_=qf_d[h, :, j * TT:(j + 1) * TT])
                       .then_inc(C["a_ld"].h, 16))
                    C["a_ld"].n += 16
                    for r in range(2):
                        ksrc = kv_ag[r * 2 * D + h * 128: r * 2 * D + (h + 1) * 128, 0:nkc]
                        em("sy", lambda e, i=ksrc, hsl=hsl, r=r, nkc=nkc:
                           e.dma_start(out=ksb[hsl][:, r * 2048:r * 2048 + nkc], in_=i)
                           .then_inc(C["a_ld"].h, 16))
                        C["a_ld"].n += 16
                        vsrc = kv_ag[r * 2 * D + D: r * 2 * D + D + nkc, h * 128:(h + 1) * 128] \
                            .rearrange("(a p) e -> p a e", p=128)
                        em("sy", lambda e, i=vsrc, hsl=hsl, r=r, nkc=nkc:
                           e.dma_start(out=vsb[hsl][:, r * 2048:r * 2048 + nkc], in_=i)
                           .then_inc(C["a_ld"].h, 16))
                        C["a_ld"].n += 16
                    msk_at = {}

                    def emit_pv(i, ext=ext, hsl=hsl):
                        masked = i >= ext - 8
                        g = C["a_pv"].n
                        gk = i // 4
                        r, lb, q4 = gk % 2, gk // 2, i % 4
                        vcol = r * 2048 + (lb * 4 + q4) * 128
                        if masked:
                            wt("te", C["a_msk"], msk_at[i])
                            pt = ptm[g % 4]
                        else:
                            wt("te", C["a_exp"], g + 1)
                            pt = ptr_[g % 4]
                        if i == 0:
                            wt("te", C["a_z"], C["a_z"].n)
                            wt("te", C["a_rbe"], C["a_rbe"].n)
                        em("te", lambda t, hsl=hsl, vcol=vcol, pt=pt, i=i, ext=ext: t.matmul(
                            bank[4][:], vsb[hsl][:, vcol:vcol + 128], pt[:],
                            start=(i == 0), stop=(i == ext - 1)))
                        em("te", lambda t, pt=pt, i=i, ext=ext: t.matmul(
                            bank[5][0:1, :], ones_col[:], pt[:],
                            start=(i == 0), stop=(i == ext - 1)).then_inc(C["a_pv"].h, 1))
                        C["a_pv"].n += 1

                    wt("te", C["a_ld"], C["a_ld"].n)
                    for i in range(ext):
                        gk = i // 4
                        r, lb, q4 = gk % 2, gk // 2, i % 4
                        kcol = r * 2048 + lb * 512 + q4 * 128
                        g = C["a_s"].n
                        if g >= 4:
                            wt("te", C["a_exp"], g - 3)
                        em("te", lambda t, hsl=hsl, kcol=kcol, g=g: t.matmul(
                            bank[g % 4][:], ksb[hsl][:, kcol:kcol + 128], qsb[hsl][:],
                            start=True, stop=True).then_inc(C["a_s"].h, 1))
                        C["a_s"].n += 1
                        wt("sc", C["a_s"], C["a_s"].n)
                        if g >= 4:
                            wt("sc", C["a_pv"], g - 3)
                        em("sc", lambda s, g=g: s.activation(
                            ptr_[g % 4][:], bank[g % 4][:], AF.Exp, bias=c_m25[:], scale=SCALE_QK)
                            .then_inc(C["a_exp"].h, 1))
                        C["a_exp"].n += 1
                        if i >= ext - 8:
                            mi = i - (ext - 8)
                            wt("ve", C["a_exp"], C["a_exp"].n)
                            if g >= 4:
                                wt("ve", C["a_pv"], g - 3)
                            em("ve", lambda v, g=g, mi=mi: v.tensor_tensor(
                                ptm[g % 4][:], ptr_[g % 4][:], masks[:, mi * TT:(mi + 1) * TT],
                                op=ALU.mult).then_inc(C["a_msk"].h, 1))
                            C["a_msk"].n += 1
                            msk_at[i] = C["a_msk"].n
                        if i >= 1:
                            emit_pv(i - 1)
                    emit_pv(ext - 1)
                    pv_hist.append(C["a_pv"].n)
                    wt("ve", C["a_pv"], C["a_pv"].n)
                    em("ve", lambda v: v.tensor_scalar(rows[:, 0:512], bank[5][0:1, :], 1.0, None, op0=ALU.mult))
                    em("ve", lambda v: v.reciprocal(rows[:, 512:1024], rows[:, 0:512]))
                    em("ve", lambda v: v.tensor_copy(rowsb[:, 0:512], rows[:, 512:1024])
                       .then_inc(C["a_den"].h, 1))
                    C["a_den"].n += 1
                    wt("te", C["a_den"], C["a_den"].n)
                    em("te", lambda t: t.matmul(bank[5][:], ones1[:], rowsb[:, 0:512], start=True, stop=True)
                       .then_inc(C["a_rb"].h, 1))
                    C["a_rb"].n += 1
                    wt("ve", C["a_rb"], C["a_rb"].n)
                    em("ve", lambda v: v.tensor_copy(rb2[:], bank[5][:]).then_inc(C["a_rbe"].h, 1))
                    C["a_rbe"].n += 1
                    em("ve", lambda v, h=h: v.tensor_tensor(
                        zall[:, h * TT:(h + 1) * TT], bank[4][:], rb2[:], op=ALU.mult)
                        .then_inc(C["a_z"].h, 1))
                    C["a_z"].n += 1
                for c in range(NCH):
                    csl = c % 2
                    gc = j * NCH + c
                    if gc >= 2:
                        wt("gp", C["a_omm"], omm_hist[gc - 2])
                    em("gp", lambda e, i=wo_in[:, :, c * 128:(c + 1) * 128], csl=csl: e.dma_start(
                        out=wob[csl][:], in_=i).then_inc(C["a_wo"].h, 16))
                    C["a_wo"].n += 16
                    wt("te", C["a_wo"], C["a_wo"].n)
                    wt("te", C["a_z"], C["a_z"].n)
                    if gc >= 2:
                        wt("te", C["a_res"], C["a_res"].n - 1)
                    for h in range(H):
                        if h == H - 1:
                            em("te", lambda t, h=h, csl=csl: t.matmul(
                                bank[6 + csl][:], wob[csl][:, h * 128:(h + 1) * 128],
                                zall[:, h * TT:(h + 1) * TT], start=False, stop=True)
                                .then_inc(C["a_omm"].h, 1))
                        else:
                            em("te", lambda t, h=h, csl=csl: t.matmul(
                                bank[6 + csl][:], wob[csl][:, h * 128:(h + 1) * 128],
                                zall[:, h * TT:(h + 1) * TT], start=(h == 0), stop=False))
                    C["a_omm"].n += 1
                    omm_hist.append(C["a_omm"].n)
                    wt("ve", C["a_omm"], C["a_omm"].n)
                    wt("ve", C["f_x"], x_mark)
                    xs = xres[:, c * TT:(c + 1) * TT]
                    em("ve", lambda v, xs=xs, csl=csl: v.tensor_tensor(
                        xs, xs, bank[6 + csl][:], op=ALU.add).then_inc(C["a_res"].h, 1))
                    C["a_res"].n += 1
                wt("gp", C["a_res"], C["a_res"].n)
                em("gp", lambda e, o=x2_d[:, :, j * TT:(j + 1) * TT]:
                   e.dma_start(out=o, in_=xres[:]).then_inc(C["f_xs"].h, 16))
                C["f_xs"].n += 16

        for _nm in ("d_st0", "d_st1", "d_st2", "d_st3"):
            wt("sy", C[_nm], C[_nm].n)
        ffn_pass(x_in, x1_d)
        p3()
        p4()
        p5()
        wt("sy", C["a_pv"], C["a_pv"].n)
        wt("sy", C["a_omm"], C["a_omm"].n)
        wt("ve", C["a_omm"], C["a_omm"].n)
        ffn_pass(x2_d, xout)
        wt("sy", C["f_xs"], C["f_xs"].n)

        with nc.Block() as block:
            @block.sync
            def _(e):
                for fn in Q["sy"]:
                    fn(e)

            @block.gpsimd
            def _(e):
                for fn in Q["gp"]:
                    fn(e)

            @block.vector
            def _(e):
                for fn in Q["ve"]:
                    fn(e)

            @block.scalar
            def _(e):
                for fn in Q["sc"]:
                    fn(e)

            @block.tensor
            def _(e):
                for fn in Q["te"]:
                    fn(e)

    return nc


def _host_prepare(inputs):
    x = np.asarray(inputs["L0_attn_x"], dtype=np.float32)
    WQ = np.asarray(inputs["L0_attn_W_Q"], dtype=np.float32)
    WK = np.asarray(inputs["L0_attn_W_K"], dtype=np.float32)
    WV = np.asarray(inputs["L0_attn_W_V"], dtype=np.float32)
    WO = np.asarray(inputs["L0_attn_W_O"], dtype=np.float32)

    def unpack(Wp):
        b = np.asarray(Wp).astype(np.uint8)
        nib = np.empty((b.shape[0], b.shape[1] * 2), np.uint8)
        nib[:, 0::2] = b & 15
        nib[:, 1::2] = (b >> 4) & 15
        return nib

    nibg = unpack(inputs["L0_ffn_gate_W_packed"])
    nibu = unpack(inputs["L0_ffn_up_W_packed"])
    nibd = unpack(inputs["L0_ffn_down_W_packed"])

    def pack_gu(nib):
        A = nib.reshape(NGR, 4, 512, NCH, 128).astype(np.uint16)
        u = A[:, 0] | (A[:, 1] << 4) | (A[:, 2] << 8) | (A[:, 3] << 12)
        return np.ascontiguousarray(u.transpose(2, 0, 3, 1))

    u16g_a = pack_gu(nibg)
    u16u_a = pack_gu(nibu)
    Y = nibd.reshape(4, 512, 128, 128).astype(np.uint16)
    u16d_a = np.ascontiguousarray(
        (Y[0] | (Y[1] << 4) | (Y[2] << 8) | (Y[3] << 12)).transpose(1, 2, 0))

    sgT = np.ascontiguousarray(np.asarray(inputs["L0_ffn_gate_scales"], dtype=np.float32).T)
    suT = np.ascontiguousarray(np.asarray(inputs["L0_ffn_up_scales"], dtype=np.float32).T)
    sdT = np.ascontiguousarray(np.asarray(inputs["L0_ffn_down_scales"], dtype=np.float32).T)

    def _whost(W):
        return np.ascontiguousarray(W.reshape(H, E, NCH, 128).transpose(0, 3, 2, 1))

    wq = _whost(WQ)
    wk = _whost(WK)
    wv = _whost(WV)
    wo = np.ascontiguousarray(WO.transpose(2, 0, 1))

    in_maps = []
    core_rows = []
    for c in range(8):
        b, p = c // 2, c % 2
        gids = [2 * jj + p for jj in range(NTT)]
        rows_idx = np.concatenate([np.arange(g * TT, (g + 1) * TT) for g in gids])
        core_rows.append((b, rows_idx))
        xc = np.ascontiguousarray(x[b][rows_idx].T.reshape(NCH, 128, TC).transpose(1, 0, 2))
        kv = (np.arange(128, dtype=np.float32)[:, None]
              + 128.0 * np.arange(8, dtype=np.float32)[None, :] - 512.0 * p)
        in_maps.append({
            "x": xc, "wq": wq, "wk": wk, "wv": wv, "wo": wo,
            "u16g": u16g_a, "u16u": u16u_a, "u16d": u16d_a,
            "sgT": sgT, "suT": suT, "sdT": sdT,
            "kvec": np.ascontiguousarray(kv),
        })
    return in_maps, core_rows


def kernel(**inputs):
    in_maps, core_rows = _host_prepare(inputs)
    nc = build_program()
    res = run_bass_kernel_spmd(nc, in_maps, core_ids=list(range(8)))
    out = np.empty((B, S, D), dtype=np.float32)
    for c in range(8):
        b, rows_idx = core_rows[c]
        xo = np.asarray(res.results[c]["xout"], dtype=np.float32)
        out[b][rows_idx] = xo.transpose(1, 0, 2).reshape(D, TC).T
    return out


# version 11
# speedup vs baseline: 1.0907x; 1.0572x over previous
import numpy as np
import ml_dtypes
from contextlib import ExitStack

import concourse.bass as bass
import concourse.mybir as mybir
from concourse.bass_utils import run_bass_kernel_spmd

F32 = mybir.dt.float32
BF16 = mybir.dt.bfloat16
U16 = mybir.dt.uint16
U32 = mybir.dt.uint32
U8 = mybir.dt.uint8
ALU = mybir.AluOpType
AF = mybir.ActivationFunctionType

B, S, D, H, E, NM = 4, 4096, 2048, 16, 128, 16384
TC = 2048
TT = 512
NTT = TC // TT
NCH = D // 128
NTN = NM // 128
NGR = 8
NPG = NTN // NGR
SCALE_QK = 1.0 / float(np.sqrt(128.0))
EXP_SHIFT = -25.0
K_G1 = 0.7978845608028654
K_G2 = K_G1 * 0.044715

ISZ = {F32: 4, BF16: 2, U16: 2, U8: 1}


class Ctr:
    def __init__(self, h):
        self.h = h
        self.n = 0


def _carve(arena, off, p, cols, dt):
    nb = cols * ISZ[dt]
    assert off % 64 == 0 and off + nb <= 134 * 1024, (off, nb)
    return arena[0:p, off:off + nb].bitcast(dt)


def build_program(debug=0):
    nc = bass.Bass("TRN2", target_bir_lowering=False)

    x_in = nc.dram_tensor("x", [128, NCH, TC], F32, kind="ExternalInput").ap()
    wq_in = nc.dram_tensor("wq", [H, 128, NCH, E], F32, kind="ExternalInput").ap()
    wk_in = nc.dram_tensor("wk", [H, 128, NCH, E], F32, kind="ExternalInput").ap()
    wv_in = nc.dram_tensor("wv", [H, 128, NCH, E], F32, kind="ExternalInput").ap()
    wo_in = nc.dram_tensor("wo", [128, H, D], F32, kind="ExternalInput").ap()
    u16g = nc.dram_tensor("u16g", [NCH, NGR, 128, 512], U16, kind="ExternalInput").ap()
    u16u = nc.dram_tensor("u16u", [NCH, NGR, 128, 512], U16, kind="ExternalInput").ap()
    u16d = nc.dram_tensor("u16d", [128, 128, 512], U16, kind="ExternalInput").ap()
    sgT = nc.dram_tensor("sgT", [64, NM], F32, kind="ExternalInput").ap()
    suT = nc.dram_tensor("suT", [64, NM], F32, kind="ExternalInput").ap()
    sdT = nc.dram_tensor("sdT", [512, D], F32, kind="ExternalInput").ap()
    kvec_in = nc.dram_tensor("kvec", [128, 8], F32, kind="ExternalInput").ap()

    xout = nc.dram_tensor("xout", [128, NCH, TC], F32, kind="ExternalOutput").ap()

    k1 = "Internal" if debug < 1 else "ExternalOutput"
    k2 = "Internal" if debug < 2 else "ExternalOutput"
    wg_d = nc.dram_tensor("wg_d", [NTN, 128, NCH, 128], BF16, kind=k2).ap()
    wu_d = nc.dram_tensor("wu_d", [NTN, 128, NCH, 128], BF16, kind=k2).ap()
    wd_d = nc.dram_tensor("wd_d", [NGR, 4, 128, NPG, 512], BF16, kind=k2).ap()
    x1_d = nc.dram_tensor("x1_d", [128, NCH, TC], F32, kind=k1).ap()
    x2_d = nc.dram_tensor("x2_d", [128, NCH, TC], F32, kind=k1).ap()
    qf_d = nc.dram_tensor("qf_d", [H, E, TC], BF16, kind=k2).ap()
    kv_loc = nc.dram_tensor("kv_loc", [2 * D, TC], BF16, kind="Internal").ap()
    kv_ag = nc.dram_tensor("kv_ag", [4 * D, TC], BF16, kind="Internal").ap()
    if debug >= 2:
        kvl_dump = nc.dram_tensor("kvl_dump", [2 * D, TC], BF16, kind="ExternalOutput").ap()
        kvag_dump = nc.dram_tensor("kvag_dump", [4 * D, TC], BF16, kind="ExternalOutput").ap()

    with ExitStack() as ctx:
        EN = ctx.enter_context

        xres = EN(nc.sbuf_tensor("xres", [128, NCH * TT], F32))
        xln = EN(nc.sbuf_tensor("xln", [128, NCH * TT], BF16))
        masks = EN(nc.sbuf_tensor("masks", [128, 8 * TT], BF16))
        iq = EN(nc.sbuf_tensor("iq", [128, TT], F32))
        kvec = EN(nc.sbuf_tensor("kvec_sb", [128, 8], F32))
        pmi = EN(nc.sbuf_tensor("pmi", [128, 128], F32))
        identf = EN(nc.sbuf_tensor("identf", [128, 128], F32))
        ones_col = EN(nc.sbuf_tensor("ones_col", [128, 1], BF16))
        ones1 = EN(nc.sbuf_tensor("ones1", [1, 128], BF16))
        c_m25 = EN(nc.sbuf_tensor("c_m25", [128, 1], F32))
        c_eps = EN(nc.sbuf_tensor("c_eps", [1, 1], F32))
        rows = EN(nc.sbuf_tensor("rows", [1, 1536], F32))
        rowsb = EN(nc.sbuf_tensor("rowsb", [1, 1024], BF16))
        arena = EN(nc.sbuf_tensor("arena", [128, 134 * 1024], U8))

        u16st = [_carve(arena, 0 + i * 1024, 128, 512, U16) for i in range(4)]
        sclf = [_carve(arena, 4096 + i * 8192, 128, 2048, F32) for i in range(4)]
        sclb = _carve(arena, 36864, 128, 2048, BF16)
        xfull = _carve(arena, 40960, 128, 2048, U16)
        dsg = _carve(arena, 45056, 128, 2048, U16)
        dt1 = _carve(arena, 49152, 128, 2048, BF16)
        decb = [_carve(arena, 53248 + i * 4096, 128, 2048, BF16) for i in range(4)]

        wgb = [_carve(arena, 0 + i * 4096, 128, 2048, BF16) for i in range(2)]
        wub = [_carve(arena, 8192 + i * 4096, 128, 2048, BF16) for i in range(2)]
        wdb = [_carve(arena, 16384 + i * 16384, 128, 8192, BF16) for i in range(2)]
        inter = _carve(arena, 49152, 128, NPG * TT, BF16)
        dacc = _carve(arena, 65536, 128, 4 * D, F32)
        xball = _carve(arena, 98304, 128, NCH * TT, BF16)
        sqs = [_carve(arena, 114688 + i * 1024, 128, 512, BF16) for i in range(2)]
        gsq = [_carve(arena, 116736 + i * 2048, 128, 512, F32) for i in range(2)]
        gav = _carve(arena, 120832, 128, 512, F32)
        ginn = [_carve(arena, 122880 + i * 2048, 128, 512, F32) for i in range(2)]
        gtha = [_carve(arena, 126976 + i * 2048, 128, 512, F32) for i in range(2)]
        uh = [_carve(arena, 131072 + i * 2048, 128, 512, F32) for i in range(2)]
        mb_sb = _carve(arena, 135168, 128, 512, BF16)
        rb_sb = _carve(arena, 136192, 128, 512, BF16)

        wqb = [_carve(arena, 0 + i * 4096, 128, 2048, BF16) for i in range(2)]
        wkb = [_carve(arena, 8192 + i * 4096, 128, 2048, BF16) for i in range(2)]
        wvb = [_carve(arena, 16384 + i * 4096, 128, 2048, BF16) for i in range(2)]
        qkt = [_carve(arena, 24576 + i * 1024, 128, 512, BF16) for i in range(6)]

        qsb = [_carve(arena, 0 + i * 1024, 128, 512, BF16) for i in range(2)]
        ksb = [_carve(arena, 2048 + i * 8192, 128, 4096, BF16) for i in range(2)]
        vsb = [_carve(arena, 18432 + i * 8192, 128, 4096, BF16) for i in range(2)]
        ptr_ = [_carve(arena, 64512 + i * 1024, 128, 512, BF16) for i in range(4)]
        ptm = [_carve(arena, 68608 + i * 1024, 128, 512, BF16) for i in range(4)]
        zall = _carve(arena, 38912, 128, H * TT, BF16)
        wob = [_carve(arena, 55296 + i * 4096, 128, 2048, BF16) for i in range(2)]
        rb2 = _carve(arena, 63488, 128, 512, BF16)

        bank = [EN(nc.psum_tensor(f"bk{i}", [128, 512], F32)) for i in range(8)]

        names = ["g0", "g1", "v0", "d_u0", "d_u1", "d_u2", "d_u3", "d_s0", "d_s1", "d_s2", "d_s3", "d_st0", "d_st1", "d_st2", "d_st3", "d_v",
                 "f_x", "f_lnv", "f_lnmm", "f_st1", "f_st2", "f_st3",
                 "f_bc", "f_ap", "f_w", "f_gmm", "f_sqa", "f_inn",
                 "f_tha", "f_epi", "f_dw", "f_dpe", "f_dadd", "f_tp",
                 "f_res", "f_xs",
                 "q_w", "q_mm", "q_ev", "q_st0", "q_st1", "c_ag",
                 "a_ld", "a_s", "a_exp", "a_msk", "a_pv", "a_den",
                 "a_rb", "a_rbe", "a_z", "a_wo", "a_omm", "a_res"]
        C = {nm: Ctr(EN(nc.semaphore(nm))) for nm in names}

        Q = {"sy": [], "gp": [], "ve": [], "sc": [], "te": []}

        def em(qn, fn):
            Q[qn].append(fn)

        def wt(qn, ctr, val):
            if val > 0:
                em(qn, lambda e, s=ctr.h, v=val: e.wait_ge(s, v))

        em("gp", lambda g: g.dma_start(out=kvec[:], in_=kvec_in).then_inc(C["g1"].h, 16))
        C["g1"].n += 16
        em("gp", lambda g: g.iota(iq[:], pattern=[[1, TT]], base=0, channel_multiplier=0,
                                  allow_small_or_imprecise_dtypes=True).then_inc(C["g0"].h, 1))
        C["g0"].n += 1
        em("gp", lambda g: g.iota(pmi[:], pattern=[[-1, 128]], base=0, channel_multiplier=1,
                                  allow_small_or_imprecise_dtypes=True).then_inc(C["g0"].h, 1))
        C["g0"].n += 1
        wt("ve", C["g0"], C["g0"].n)
        wt("ve", C["g1"], C["g1"].n)
        for i in range(8):
            em("ve", lambda v, i=i: v.tensor_scalar(
                masks[:, i * TT:(i + 1) * TT], iq[:], kvec[:, i:i + 1], None, op0=ALU.is_ge))
        em("ve", lambda v: v.tensor_scalar(identf[:], pmi[:], 0.0, None, op0=ALU.is_equal))
        em("ve", lambda v: v.memset(ones_col[:], 1.0))
        em("ve", lambda v: v.memset(ones1[:], 1.0))
        em("ve", lambda v: v.memset(c_m25[:], EXP_SHIFT))
        em("ve", lambda v: v.memset(c_eps[:], 1e-5).then_inc(C["v0"].h, 1))
        C["v0"].n += 1
        wt("sc", C["v0"], 1)
        wt("te", C["v0"], 1)

        blocks = []
        for G in range(NGR):
            blocks += [("g", c, G) for c in range(NCH)]
            blocks += [("u", c, G) for c in range(NCH)]
            blocks += [("d", G * NPG + j, G) for j in range(NPG)]
        NBLK = len(blocks)

        for k, (m, a, G) in enumerate(blocks):
            m_idx = {"g": 0, "u": 1, "d": 2}[m]
            if k >= 4:
                wt("sy", C["d_v"], k - 3)
            src = (u16g[a, G] if m == "g" else u16u[a, G] if m == "u" else u16d[a])
            em("sy", lambda e, o=u16st[k % 4], i=src, k=k:
               e.dma_start(out=o, in_=i).then_inc(C[f"d_u{k % 4}"].h, 16))
            C[f"d_u{k % 4}"].n += 16
            c_in = a % NPG if m == "d" else a
            if m == "g":
                ssrc = sgT[4 * c_in:4 * c_in + 4, G * 2048:(G + 1) * 2048]
            elif m == "u":
                ssrc = suT[4 * c_in:4 * c_in + 4, G * 2048:(G + 1) * 2048]
            else:
                ssrc = sdT[4 * a:4 * a + 4, :]
            if k >= 4:
                wt("gp", C["d_v"], k - 3)
            em("gp", lambda e, o=sclf[k % 4], i=ssrc, k=k:
               e.dma_start(out=o, in_=i.unsqueeze(1).broadcast_to([4, 32, 2048])).then_inc(C[f"d_s{k % 4}"].h, 16))
            C[f"d_s{k % 4}"].n += 16
            wt("ve", C[f"d_u{k % 4}"], 16 * (k // 4 + 1))
            u32s = u16st[k % 4].bitcast(U32)
            xf32 = xfull.bitcast(U32)
            sg32v = dsg.bitcast(U32)
            em("ve", lambda v, u=u32s: v.tensor_scalar(xf32[:, 0:256], u, 6, 0x01C001C0,
                                                       op0=ALU.logical_shift_left, op1=ALU.bitwise_and))
            em("ve", lambda v, u=u32s: v.tensor_scalar(xf32[:, 256:512], u, 2, 0x01C001C0,
                                                       op0=ALU.logical_shift_left, op1=ALU.bitwise_and))
            em("ve", lambda v, u=u32s: v.tensor_scalar(xf32[:, 512:768], u, 2, 0x01C001C0,
                                                       op0=ALU.logical_shift_right, op1=ALU.bitwise_and))
            em("ve", lambda v, u=u32s: v.tensor_scalar(xf32[:, 768:1024], u, 6, 0x01C001C0,
                                                       op0=ALU.logical_shift_right, op1=ALU.bitwise_and))
            em("ve", lambda v, u=u32s: v.tensor_scalar(sg32v[:, 0:256], u, 12, 0x80008000,
                                                       op0=ALU.logical_shift_left, op1=ALU.bitwise_and))
            em("ve", lambda v, u=u32s: v.tensor_scalar(sg32v[:, 256:512], u, 8, 0x80008000,
                                                       op0=ALU.logical_shift_left, op1=ALU.bitwise_and))
            em("ve", lambda v, u=u32s: v.tensor_scalar(sg32v[:, 512:768], u, 4, 0x80008000,
                                                       op0=ALU.logical_shift_left, op1=ALU.bitwise_and))
            em("ve", lambda v, u=u32s: v.tensor_scalar(sg32v[:, 768:1024], u, 0, 0x80008000,
                                                       op0=ALU.logical_shift_left, op1=ALU.bitwise_and))
            em("ve", lambda v: v.tensor_scalar(xfull[:], xfull[:], 0x3F00, None, op0=ALU.add))
            em("ve", lambda v: v.tensor_scalar(dt1[:], xfull[:].bitcast(BF16), 2.0, -1.0,
                                               op0=ALU.mult, op1=ALU.add))
            em("ve", lambda v: v.tensor_tensor(dt1[:], xfull[:].bitcast(BF16), dt1[:], op=ALU.min))
            wt("ve", C[f"d_s{k % 4}"], 16 * (k // 4 + 1))
            em("ve", lambda v, s=sclf[k % 4]: v.tensor_copy(sclb[:], s))
            em("ve", lambda v: v.tensor_tensor(
                dsg[:].bitcast(U32), sclb[:].bitcast(U32), dsg[:].bitcast(U32), op=ALU.bitwise_or))
            if k >= 4:
                wt("ve", C[f"d_st{k % 4}"], 16 * (k // 4))
            em("ve", lambda v, o=decb[k % 4]: v.tensor_tensor(
                o, dt1[:], dsg[:].bitcast(BF16), op=ALU.mult).then_inc(C["d_v"].h, 1))
            C["d_v"].n += 1
            wt("sc", C["d_v"], k + 1)
            if m in ("g", "u"):
                dst = (wg_d if m == "g" else wu_d)[G * NPG:(G + 1) * NPG, :, a, :] \
                    .rearrange("a p b -> p a b")
            else:
                dst = wd_d[a // NPG, :, :, a % NPG, :].rearrange("a p b -> p a b")
            em("sc", lambda e, o=dst, i=decb[k % 4], k=k:
               e.dma_start(out=o, in_=i).then_inc(C[f"d_st{k % 4}"].h, 16))
            C[f"d_st{k % 4}"].n += 16

        def ln_tile():
            wt("ve", C["f_x"], C["f_x"].n)
            wt("ve", C["f_lnmm"], C["f_lnmm"].n)
            for c in range(NCH):
                em("ve", lambda v, c=c: v.tensor_copy(
                    xball[:, c * TT:(c + 1) * TT], xres[:, c * TT:(c + 1) * TT]))
                if C["f_lnmm"].n >= 1:
                    wt("ve", C["f_lnmm"], C["f_lnmm"].n - 1)
                em("ve", lambda v, c=c: v.tensor_tensor(
                    sqs[c % 2], xball[:, c * TT:(c + 1) * TT],
                    xball[:, c * TT:(c + 1) * TT], op=ALU.mult).then_inc(C["f_lnv"].h, 1))
                C["f_lnv"].n += 1
                wt("te", C["f_lnv"], C["f_lnv"].n)
                if c == 0:
                    wt("te", C["f_res"], C["f_res"].n)
                    wt("te", C["a_res"], C["a_res"].n)
                em("te", lambda t, c=c: t.matmul(
                    bank[6][0:1, :], ones_col[:], xball[:, c * TT:(c + 1) * TT],
                    start=(c == 0), stop=(c == NCH - 1)))
                em("te", lambda t, c=c: t.matmul(
                    bank[7][0:1, :], ones_col[:], sqs[c % 2],
                    start=(c == 0), stop=(c == NCH - 1)).then_inc(C["f_lnmm"].h, 1))
                C["f_lnmm"].n += 1
            wt("ve", C["f_lnmm"], C["f_lnmm"].n)
            em("ve", lambda v: v.tensor_scalar(rows[:, 0:512], bank[6][0:1, :], 1.0 / 2048.0, None, op0=ALU.mult))
            em("ve", lambda v: v.tensor_scalar(rows[:, 512:1024], bank[7][0:1, :], 1.0 / 2048.0, None, op0=ALU.mult))
            em("ve", lambda v: v.tensor_tensor(rows[:, 1024:1536], rows[:, 0:512], rows[:, 0:512], op=ALU.mult))
            em("ve", lambda v: v.tensor_tensor(
                rows[:, 512:1024], rows[:, 512:1024], rows[:, 1024:1536], op=ALU.subtract)
                .then_inc(C["f_st1"].h, 1))
            C["f_st1"].n += 1
            wt("sc", C["f_st1"], C["f_st1"].n)
            em("sc", lambda s: s.activation(rows[:, 1024:1536], rows[:, 512:1024], AF.Sqrt,
                                            bias=c_eps[:], scale=1.0).then_inc(C["f_st2"].h, 1))
            C["f_st2"].n += 1
            wt("ve", C["f_st2"], C["f_st2"].n)
            em("ve", lambda v: v.reciprocal(rows[:, 512:1024], rows[:, 1024:1536]))
            em("ve", lambda v: v.tensor_copy(rowsb[:, 0:512], rows[:, 0:512]))
            em("ve", lambda v: v.tensor_copy(rowsb[:, 512:1024], rows[:, 512:1024]).then_inc(C["f_st3"].h, 1))
            C["f_st3"].n += 1
            wt("te", C["f_st3"], C["f_st3"].n)
            em("te", lambda t: t.matmul(bank[6][:], ones1[:], rowsb[:, 0:512], start=True, stop=True))
            em("te", lambda t: t.matmul(bank[7][:], ones1[:], rowsb[:, 512:1024], start=True, stop=True)
               .then_inc(C["f_bc"].h, 1))
            C["f_bc"].n += 1
            wt("ve", C["f_bc"], C["f_bc"].n)
            em("ve", lambda v: v.tensor_copy(mb_sb[:], bank[6][:]))
            em("ve", lambda v: v.tensor_copy(rb_sb[:], bank[7][:]))
            wt("ve", C["f_gmm"], C["f_gmm"].n)
            wt("ve", C["q_mm"], C["q_mm"].n)
            for c in range(NCH):
                em("ve", lambda v, c=c: v.tensor_tensor(
                    xball[:, c * TT:(c + 1) * TT], xball[:, c * TT:(c + 1) * TT], mb_sb[:], op=ALU.subtract))
                em("ve", lambda v, c=c: v.tensor_tensor(
                    xln[:, c * TT:(c + 1) * TT], xball[:, c * TT:(c + 1) * TT], rb_sb[:], op=ALU.mult)
                    .then_inc(C["f_ap"].h, 1))
                C["f_ap"].n += 1

        def ffn_pass(src_ap, dst_ap):
            for ti in range(NTT):
                wt("gp", C["f_xs"], C["f_xs"].n)
                em("gp", lambda e, i=src_ap[:, :, ti * TT:(ti + 1) * TT]:
                   e.dma_start(out=xres[:], in_=i).then_inc(C["f_x"].h, 16))
                C["f_x"].n += 16
                ln_tile()
                for g in range(NGR):
                    for j in range(NPG):
                        jn = g * NPG + j
                        sl = jn % 2
                        if C["f_gmm"].n >= 1:
                            wt("sy", C["f_gmm"], C["f_gmm"].n - 1)
                        em("sy", lambda e, i=wg_d[jn], o=wgb[sl]: e.dma_start(
                            out=o, in_=i).then_inc(C["f_w"].h, 16))
                        C["f_w"].n += 16
                        em("sy", lambda e, i=wu_d[jn], o=wub[sl]: e.dma_start(
                            out=o, in_=i).then_inc(C["f_w"].h, 16))
                        C["f_w"].n += 16
                        wt("te", C["f_w"], C["f_w"].n)
                        wt("te", C["f_ap"], C["f_ap"].n)
                        if C["f_epi"].n >= 1:
                            wt("te", C["f_epi"], C["f_epi"].n - 1)
                        for c in range(NCH):
                            em("te", lambda t, c=c, sl=sl: t.matmul(
                                bank[sl][:], wgb[sl][:, c * 128:(c + 1) * 128],
                                xln[:, c * TT:(c + 1) * TT], start=(c == 0), stop=(c == NCH - 1)))
                        for c in range(NCH):
                            if c == NCH - 1:
                                em("te", lambda t, c=c, sl=sl: t.matmul(
                                    bank[2 + sl][:], wub[sl][:, c * 128:(c + 1) * 128],
                                    xln[:, c * TT:(c + 1) * TT], start=False, stop=True)
                                    .then_inc(C["f_gmm"].h, 1))
                            else:
                                em("te", lambda t, c=c, sl=sl: t.matmul(
                                    bank[2 + sl][:], wub[sl][:, c * 128:(c + 1) * 128],
                                    xln[:, c * TT:(c + 1) * TT], start=(c == 0), stop=False))
                        C["f_gmm"].n += 1
                        wt("sc", C["f_gmm"], C["f_gmm"].n)
                        em("sc", lambda s, sl=sl: s.activation(gsq[sl], bank[sl][:], AF.Square))
                        em("sc", lambda s, sl=sl: s.activation(uh[sl], bank[2 + sl][:], AF.Copy, scale=0.5)
                           .then_inc(C["f_sqa"].h, 1))
                        C["f_sqa"].n += 1
                        wt("ve", C["f_sqa"], C["f_sqa"].n)
                        if g == 0 and j == 0:
                            wt("ve", C["f_dpe"], C["f_dpe"].n)
                            wt("ve", C["a_omm"], C["a_omm"].n)
                        if j == 0:
                            wt("ve", C["f_dpe"], C["f_dpe"].n)
                        em("ve", lambda v, sl=sl: v.tensor_scalar(gav[:], gsq[sl], K_G2, K_G1,
                                                                  op0=ALU.mult, op1=ALU.add))
                        em("ve", lambda v, sl=sl: v.tensor_tensor(ginn[sl], gav[:], bank[sl][:], op=ALU.mult)
                           .then_inc(C["f_inn"].h, 1))
                        C["f_inn"].n += 1
                        wt("sc", C["f_inn"], C["f_inn"].n)
                        em("sc", lambda s, sl=sl: s.activation(gtha[sl], ginn[sl], AF.Tanh)
                           .then_inc(C["f_tha"].h, 1))
                        C["f_tha"].n += 1
                        wt("ve", C["f_tha"], C["f_tha"].n)
                        em("ve", lambda v, sl=sl: v.tensor_tensor(gav[:], bank[sl][:], uh[sl], op=ALU.mult))
                        em("ve", lambda v, sl=sl, j=j: v.scalar_tensor_tensor(
                            inter[:, j * TT:(j + 1) * TT], gtha[sl], 1.0, gav[:],
                            op0=ALU.add, op1=ALU.mult).then_inc(C["f_epi"].h, 1))
                        C["f_epi"].n += 1
                    for dt in range(4):
                        dsl = dt % 2
                        if C["f_dpe"].n >= 4:
                            wt("sy", C["f_dpe"], C["f_dpe"].n - 4)
                        em("sy", lambda e, i=wd_d[g, dt], o=wdb[dsl]: e.dma_start(
                            out=o, in_=i).then_inc(C["f_dw"].h, 16))
                        C["f_dw"].n += 16
                        wt("te", C["f_dw"], C["f_dw"].n)
                        wt("te", C["f_epi"], C["f_epi"].n)
                        for tsub in range(4):
                            bsl = 4 + (tsub % 2)
                            if C["f_dadd"].n >= 1:
                                wt("te", C["f_dadd"], C["f_dadd"].n - 1)
                            for j in range(NPG):
                                if j == NPG - 1:
                                    em("te", lambda t, j=j, tsub=tsub, dsl=dsl, bsl=bsl: t.matmul(
                                        bank[bsl][:],
                                        inter[:, j * TT + tsub * 128: j * TT + tsub * 128 + 128],
                                        wdb[dsl][:, j * 512:(j + 1) * 512],
                                        start=False, stop=True).then_inc(C["f_dpe"].h, 1))
                                else:
                                    em("te", lambda t, j=j, tsub=tsub, dsl=dsl, bsl=bsl: t.matmul(
                                        bank[bsl][:],
                                        inter[:, j * TT + tsub * 128: j * TT + tsub * 128 + 128],
                                        wdb[dsl][:, j * 512:(j + 1) * 512],
                                        start=(j == 0), stop=False))
                            C["f_dpe"].n += 1
                            wt("ve", C["f_dpe"], C["f_dpe"].n)
                            if g == 0:
                                wt("ve", C["f_tp"], C["f_tp"].n)
                            dslice = dacc[:, tsub * D + dt * 512: tsub * D + dt * 512 + 512]
                            if g == 0:
                                em("ve", lambda v, o=dslice, b=bank[bsl]: v.tensor_copy(o, b[:])
                                   .then_inc(C["f_dadd"].h, 1))
                            else:
                                em("ve", lambda v, o=dslice, b=bank[bsl]: v.tensor_tensor(
                                    o, o, b[:], op=ALU.add).then_inc(C["f_dadd"].h, 1))
                            C["f_dadd"].n += 1
                wt("te", C["f_dadd"], C["f_dadd"].n)
                wt("te", C["f_ap"], C["f_ap"].n)
                for tsub in range(4):
                    for c in range(NCH):
                        tb_ = 6 + ((tsub * NCH + c) % 2)
                        if C["f_tp"].n >= 2:
                            wt("te", C["f_res"], C["f_tp"].n - 1)
                        em("te", lambda t, tsub=tsub, c=c, tb_=tb_: t.transpose(
                            bank[tb_][:, 0:128],
                            dacc[:, tsub * D + c * 128: tsub * D + c * 128 + 128], identf[:])
                            .then_inc(C["f_tp"].h, 1))
                        C["f_tp"].n += 1
                        wt("ve", C["f_tp"], C["f_tp"].n)
                        xslice = xres[:, c * TT + tsub * 128: c * TT + tsub * 128 + 128]
                        em("ve", lambda v, x=xslice, tb_=tb_: v.scalar_tensor_tensor(
                            x, bank[tb_][:, 0:128], 0.5, x, op0=ALU.mult, op1=ALU.add)
                            .then_inc(C["f_res"].h, 1))
                        C["f_res"].n += 1
                wt("gp", C["f_res"], C["f_res"].n)
                em("gp", lambda e, o=dst_ap[:, :, ti * TT:(ti + 1) * TT]:
                   e.dma_start(out=o, in_=xres[:]).then_inc(C["f_xs"].h, 16))
                C["f_xs"].n += 16

        def p3():
            wt("gp", C["f_gmm"], C["f_gmm"].n)
            wt("gp", C["f_dpe"], C["f_dpe"].n)
            wt("ve", C["f_dpe"], C["f_dpe"].n)
            qmm_hist = []
            for ti in range(NTT):
                wt("gp", C["f_xs"], C["f_xs"].n)
                wt("gp", C["f_ap"], C["f_ap"].n)
                em("gp", lambda e, i=x1_d[:, :, ti * TT:(ti + 1) * TT]:
                   e.dma_start(out=xres[:], in_=i).then_inc(C["f_x"].h, 16))
                C["f_x"].n += 16
                ln_tile()
                for h in range(H):
                    hsl = h % 2
                    gi = ti * H + h
                    if gi >= 2:
                        wt("gp", C["q_mm"], qmm_hist[gi - 2])
                    for (w_in, buf) in ((wq_in, wqb), (wk_in, wkb), (wv_in, wvb)):
                        em("gp", lambda e, i=w_in[h], o=buf[hsl]: e.dma_start(
                            out=o, in_=i).then_inc(C["q_w"].h, 16))
                        C["q_w"].n += 16
                    wt("te", C["q_w"], C["q_w"].n)
                    wt("te", C["f_ap"], C["f_ap"].n)
                    if gi >= 2:
                        wt("te", C["q_ev"], C["q_ev"].n - 1)
                    for c in range(NCH):
                        em("te", lambda t, c=c, hsl=hsl: t.matmul(
                            bank[3 * hsl][:], wqb[hsl][:, c * 128:(c + 1) * 128],
                            xln[:, c * TT:(c + 1) * TT], start=(c == 0), stop=(c == NCH - 1)))
                    for c in range(NCH):
                        em("te", lambda t, c=c, hsl=hsl: t.matmul(
                            bank[3 * hsl + 1][:], wkb[hsl][:, c * 128:(c + 1) * 128],
                            xln[:, c * TT:(c + 1) * TT], start=(c == 0), stop=(c == NCH - 1)))
                    for c in range(NCH):
                        for ks in range(4):
                            last = (c == NCH - 1 and ks == 3)
                            if last:
                                em("te", lambda t, c=c, ks=ks, hsl=hsl: t.matmul(
                                    bank[3 * hsl + 2][:, ks * 128:(ks + 1) * 128],
                                    xln[:, c * TT + ks * 128: c * TT + ks * 128 + 128],
                                    wvb[hsl][:, c * 128:(c + 1) * 128],
                                    start=False, stop=True).then_inc(C["q_mm"].h, 1))
                            else:
                                em("te", lambda t, c=c, ks=ks, hsl=hsl: t.matmul(
                                    bank[3 * hsl + 2][:, ks * 128:(ks + 1) * 128],
                                    xln[:, c * TT + ks * 128: c * TT + ks * 128 + 128],
                                    wvb[hsl][:, c * 128:(c + 1) * 128],
                                    start=(c == 0 and ks == 0), stop=False))
                    C["q_mm"].n += 1
                    qmm_hist.append(C["q_mm"].n)
                    wt("ve", C["q_mm"], C["q_mm"].n)
                    if gi >= 2:
                        wt("ve", C[f"q_st{gi % 2}"], 48 * (gi // 2))
                    em("ve", lambda v, hsl=hsl: v.tensor_copy(qkt[3 * hsl][:], bank[3 * hsl][:]))
                    em("ve", lambda v, hsl=hsl: v.tensor_copy(qkt[3 * hsl + 1][:], bank[3 * hsl + 1][:]))
                    em("ve", lambda v, hsl=hsl: v.tensor_copy(qkt[3 * hsl + 2][:], bank[3 * hsl + 2][:])
                       .then_inc(C["q_ev"].h, 1))
                    C["q_ev"].n += 1
                    wt("sy", C["q_ev"], C["q_ev"].n)
                    em("sy", lambda e, o=qf_d[h, :, ti * TT:(ti + 1) * TT], i=qkt[3 * hsl][:], gi=gi:
                       e.dma_start(out=o, in_=i).then_inc(C[f"q_st{gi % 2}"].h, 16))
                    C[f"q_st{gi % 2}"].n += 16
                    em("sy", lambda e, o=kv_loc[h * 128:(h + 1) * 128, ti * TT:(ti + 1) * TT],
                       i=qkt[3 * hsl + 1][:], gi=gi: e.dma_start(out=o, in_=i).then_inc(C[f"q_st{gi % 2}"].h, 16))
                    C[f"q_st{gi % 2}"].n += 16
                    vdst = kv_loc[D + ti * TT: D + (ti + 1) * TT, h * 128:(h + 1) * 128] \
                        .rearrange("(a p) e -> p a e", p=128)
                    em("sy", lambda e, o=vdst, i=qkt[3 * hsl + 2][:], gi=gi:
                       e.dma_start(out=o, in_=i).then_inc(C[f"q_st{gi % 2}"].h, 16))
                    C[f"q_st{gi % 2}"].n += 16

        def p4():
            wt("gp", C["q_st0"], C["q_st0"].n)
            wt("gp", C["q_st1"], C["q_st1"].n)
            em("gp", lambda g: g.collective_compute(
                "AllGather", ALU.bypass,
                replica_groups=[[0, 1], [2, 3], [4, 5], [6, 7]],
                ins=[kv_loc.opt()],
                outs=[kv_ag.opt()],
            ).then_inc(C["c_ag"].h, 1))
            C["c_ag"].n += 1
            if debug >= 2:
                wt("gp", C["c_ag"], C["c_ag"].n)
                em("gp", lambda g: g.dma_start(out=kvl_dump, in_=kv_loc).then_inc(C["g0"].h, 16))
                C["g0"].n += 16
                em("gp", lambda g: g.dma_start(out=kvag_dump, in_=kv_ag).then_inc(C["g0"].h, 16))
                C["g0"].n += 16
                wt("gp", C["g0"], C["g0"].n)

        def p5():
            wt("sy", C["c_ag"], C["c_ag"].n)
            wt("sy", C["q_mm"], C["q_mm"].n)
            wt("ve", C["f_dpe"], C["f_dpe"].n)
            pv_hist = []
            omm_hist = []
            for j in range(NTT):
                wt("gp", C["a_res"], C["a_res"].n)
                wt("gp", C["f_xs"], C["f_xs"].n)
                em("gp", lambda e, i=x1_d[:, :, j * TT:(j + 1) * TT]:
                   e.dma_start(out=xres[:], in_=i).then_inc(C["f_x"].h, 16))
                C["f_x"].n += 16
                x_mark = C["f_x"].n
                ext = 8 * j + 8
                nkc = (j + 1) * 512
                for h in range(H):
                    hsl = h % 2
                    gh = j * H + h
                    if gh >= 2:
                        wt("sy", C["a_pv"], pv_hist[gh - 2])
                    em("sy", lambda e, hsl=hsl, h=h, j=j:
                       e.dma_start(out=qsb[hsl][:], in_=qf_d[h, :, j * TT:(j + 1) * TT])
                       .then_inc(C["a_ld"].h, 16))
                    C["a_ld"].n += 16
                    for r in range(2):
                        ksrc = kv_ag[r * 2 * D + h * 128: r * 2 * D + (h + 1) * 128, 0:nkc]
                        em("sy", lambda e, i=ksrc, hsl=hsl, r=r, nkc=nkc:
                           e.dma_start(out=ksb[hsl][:, r * 2048:r * 2048 + nkc], in_=i)
                           .then_inc(C["a_ld"].h, 16))
                        C["a_ld"].n += 16
                        vsrc = kv_ag[r * 2 * D + D: r * 2 * D + D + nkc, h * 128:(h + 1) * 128] \
                            .rearrange("(a p) e -> p a e", p=128)
                        em("sy", lambda e, i=vsrc, hsl=hsl, r=r, nkc=nkc:
                           e.dma_start(out=vsb[hsl][:, r * 2048:r * 2048 + nkc], in_=i)
                           .then_inc(C["a_ld"].h, 16))
                        C["a_ld"].n += 16
                    msk_at = {}

                    def emit_pv(i, ext=ext, hsl=hsl):
                        masked = i >= ext - 8
                        g = C["a_pv"].n
                        gk = i // 4
                        r, lb, q4 = gk % 2, gk // 2, i % 4
                        vcol = r * 2048 + (lb * 4 + q4) * 128
                        if masked:
                            wt("te", C["a_msk"], msk_at[i])
                            pt = ptm[g % 4]
                        else:
                            wt("te", C["a_exp"], g + 1)
                            pt = ptr_[g % 4]
                        if i == 0:
                            wt("te", C["a_z"], C["a_z"].n)
                            wt("te", C["a_rbe"], C["a_rbe"].n)
                        em("te", lambda t, hsl=hsl, vcol=vcol, pt=pt, i=i, ext=ext: t.matmul(
                            bank[4][:], vsb[hsl][:, vcol:vcol + 128], pt[:],
                            start=(i == 0), stop=(i == ext - 1)))
                        em("te", lambda t, pt=pt, i=i, ext=ext: t.matmul(
                            bank[5][0:1, :], ones_col[:], pt[:],
                            start=(i == 0), stop=(i == ext - 1)).then_inc(C["a_pv"].h, 1))
                        C["a_pv"].n += 1

                    wt("te", C["a_ld"], C["a_ld"].n)
                    for i in range(ext):
                        gk = i // 4
                        r, lb, q4 = gk % 2, gk // 2, i % 4
                        kcol = r * 2048 + lb * 512 + q4 * 128
                        g = C["a_s"].n
                        if g >= 4:
                            wt("te", C["a_exp"], g - 3)
                        em("te", lambda t, hsl=hsl, kcol=kcol, g=g: t.matmul(
                            bank[g % 4][:], ksb[hsl][:, kcol:kcol + 128], qsb[hsl][:],
                            start=True, stop=True).then_inc(C["a_s"].h, 1))
                        C["a_s"].n += 1
                        wt("sc", C["a_s"], C["a_s"].n)
                        if g >= 4:
                            wt("sc", C["a_pv"], g - 3)
                        em("sc", lambda s, g=g: s.activation(
                            ptr_[g % 4][:], bank[g % 4][:], AF.Exp, bias=c_m25[:], scale=SCALE_QK)
                            .then_inc(C["a_exp"].h, 1))
                        C["a_exp"].n += 1
                        if i >= ext - 8:
                            mi = i - (ext - 8)
                            wt("ve", C["a_exp"], C["a_exp"].n)
                            if g >= 4:
                                wt("ve", C["a_pv"], g - 3)
                            em("ve", lambda v, g=g, mi=mi: v.tensor_tensor(
                                ptm[g % 4][:], ptr_[g % 4][:], masks[:, mi * TT:(mi + 1) * TT],
                                op=ALU.mult).then_inc(C["a_msk"].h, 1))
                            C["a_msk"].n += 1
                            msk_at[i] = C["a_msk"].n
                        if i >= 1:
                            emit_pv(i - 1)
                    emit_pv(ext - 1)
                    pv_hist.append(C["a_pv"].n)
                    wt("ve", C["a_pv"], C["a_pv"].n)
                    em("ve", lambda v: v.tensor_scalar(rows[:, 0:512], bank[5][0:1, :], 1.0, None, op0=ALU.mult))
                    em("ve", lambda v: v.reciprocal(rows[:, 512:1024], rows[:, 0:512]))
                    em("ve", lambda v: v.tensor_copy(rowsb[:, 0:512], rows[:, 512:1024])
                       .then_inc(C["a_den"].h, 1))
                    C["a_den"].n += 1
                    wt("te", C["a_den"], C["a_den"].n)
                    em("te", lambda t: t.matmul(bank[5][:], ones1[:], rowsb[:, 0:512], start=True, stop=True)
                       .then_inc(C["a_rb"].h, 1))
                    C["a_rb"].n += 1
                    wt("ve", C["a_rb"], C["a_rb"].n)
                    em("ve", lambda v: v.tensor_copy(rb2[:], bank[5][:]).then_inc(C["a_rbe"].h, 1))
                    C["a_rbe"].n += 1
                    em("ve", lambda v, h=h: v.tensor_tensor(
                        zall[:, h * TT:(h + 1) * TT], bank[4][:], rb2[:], op=ALU.mult)
                        .then_inc(C["a_z"].h, 1))
                    C["a_z"].n += 1
                for c in range(NCH):
                    csl = c % 2
                    gc = j * NCH + c
                    if gc >= 2:
                        wt("gp", C["a_omm"], omm_hist[gc - 2])
                    em("gp", lambda e, i=wo_in[:, :, c * 128:(c + 1) * 128], csl=csl: e.dma_start(
                        out=wob[csl][:], in_=i).then_inc(C["a_wo"].h, 16))
                    C["a_wo"].n += 16
                    wt("te", C["a_wo"], C["a_wo"].n)
                    wt("te", C["a_z"], C["a_z"].n)
                    if gc >= 2:
                        wt("te", C["a_res"], C["a_res"].n - 1)
                    for h in range(H):
                        if h == H - 1:
                            em("te", lambda t, h=h, csl=csl: t.matmul(
                                bank[6 + csl][:], wob[csl][:, h * 128:(h + 1) * 128],
                                zall[:, h * TT:(h + 1) * TT], start=False, stop=True)
                                .then_inc(C["a_omm"].h, 1))
                        else:
                            em("te", lambda t, h=h, csl=csl: t.matmul(
                                bank[6 + csl][:], wob[csl][:, h * 128:(h + 1) * 128],
                                zall[:, h * TT:(h + 1) * TT], start=(h == 0), stop=False))
                    C["a_omm"].n += 1
                    omm_hist.append(C["a_omm"].n)
                    wt("ve", C["a_omm"], C["a_omm"].n)
                    wt("ve", C["f_x"], x_mark)
                    xs = xres[:, c * TT:(c + 1) * TT]
                    em("ve", lambda v, xs=xs, csl=csl: v.tensor_tensor(
                        xs, xs, bank[6 + csl][:], op=ALU.add).then_inc(C["a_res"].h, 1))
                    C["a_res"].n += 1
                wt("gp", C["a_res"], C["a_res"].n)
                em("gp", lambda e, o=x2_d[:, :, j * TT:(j + 1) * TT]:
                   e.dma_start(out=o, in_=xres[:]).then_inc(C["f_xs"].h, 16))
                C["f_xs"].n += 16

        for _nm in ("d_st0", "d_st1", "d_st2", "d_st3"):
            for _q in ("sy", "gp", "te", "ve", "sc"):
                wt(_q, C[_nm], C[_nm].n)
        for _q in ("sy", "gp", "te", "sc"):
            wt(_q, C["d_v"], C["d_v"].n)
        ffn_pass(x_in, x1_d)
        p3()
        p4()
        p5()
        wt("sy", C["a_pv"], C["a_pv"].n)
        wt("sy", C["a_omm"], C["a_omm"].n)
        wt("ve", C["a_omm"], C["a_omm"].n)
        ffn_pass(x2_d, xout)
        wt("sy", C["f_xs"], C["f_xs"].n)

        with nc.Block() as block:
            @block.sync
            def _(e):
                for fn in Q["sy"]:
                    fn(e)

            @block.gpsimd
            def _(e):
                for fn in Q["gp"]:
                    fn(e)

            @block.vector
            def _(e):
                for fn in Q["ve"]:
                    fn(e)

            @block.scalar
            def _(e):
                for fn in Q["sc"]:
                    fn(e)

            @block.tensor
            def _(e):
                for fn in Q["te"]:
                    fn(e)

    return nc


def _host_prepare(inputs):
    x = np.asarray(inputs["L0_attn_x"], dtype=np.float32)
    WQ = np.asarray(inputs["L0_attn_W_Q"], dtype=np.float32)
    WK = np.asarray(inputs["L0_attn_W_K"], dtype=np.float32)
    WV = np.asarray(inputs["L0_attn_W_V"], dtype=np.float32)
    WO = np.asarray(inputs["L0_attn_W_O"], dtype=np.float32)

    def unpack(Wp):
        b = np.asarray(Wp).astype(np.uint8)
        nib = np.empty((b.shape[0], b.shape[1] * 2), np.uint8)
        nib[:, 0::2] = b & 15
        nib[:, 1::2] = (b >> 4) & 15
        return nib

    nibg = unpack(inputs["L0_ffn_gate_W_packed"])
    nibu = unpack(inputs["L0_ffn_up_W_packed"])
    nibd = unpack(inputs["L0_ffn_down_W_packed"])

    def pack_gu(nib):
        A = nib.reshape(NGR, 4, 512, NCH, 128).astype(np.uint16)
        u = A[:, 0] | (A[:, 1] << 4) | (A[:, 2] << 8) | (A[:, 3] << 12)
        return np.ascontiguousarray(u.transpose(2, 0, 3, 1))

    u16g_a = pack_gu(nibg)
    u16u_a = pack_gu(nibu)
    Y = nibd.reshape(4, 512, 128, 128).astype(np.uint16)
    u16d_a = np.ascontiguousarray(
        (Y[0] | (Y[1] << 4) | (Y[2] << 8) | (Y[3] << 12)).transpose(1, 2, 0))

    sgT = np.ascontiguousarray(np.asarray(inputs["L0_ffn_gate_scales"], dtype=np.float32).T)
    suT = np.ascontiguousarray(np.asarray(inputs["L0_ffn_up_scales"], dtype=np.float32).T)
    sdT = np.ascontiguousarray(np.asarray(inputs["L0_ffn_down_scales"], dtype=np.float32).T)

    def _whost(W):
        return np.ascontiguousarray(W.reshape(H, E, NCH, 128).transpose(0, 3, 2, 1))

    wq = _whost(WQ)
    wk = _whost(WK)
    wv = _whost(WV)
    wo = np.ascontiguousarray(WO.transpose(2, 0, 1))

    in_maps = []
    core_rows = []
    for c in range(8):
        b, p = c // 2, c % 2
        gids = [2 * jj + p for jj in range(NTT)]
        rows_idx = np.concatenate([np.arange(g * TT, (g + 1) * TT) for g in gids])
        core_rows.append((b, rows_idx))
        xc = np.ascontiguousarray(x[b][rows_idx].T.reshape(NCH, 128, TC).transpose(1, 0, 2))
        kv = (np.arange(128, dtype=np.float32)[:, None]
              + 128.0 * np.arange(8, dtype=np.float32)[None, :] - 512.0 * p)
        in_maps.append({
            "x": xc, "wq": wq, "wk": wk, "wv": wv, "wo": wo,
            "u16g": u16g_a, "u16u": u16u_a, "u16d": u16d_a,
            "sgT": sgT, "suT": suT, "sdT": sdT,
            "kvec": np.ascontiguousarray(kv),
        })
    return in_maps, core_rows


def kernel(**inputs):
    in_maps, core_rows = _host_prepare(inputs)
    nc = build_program()
    res = run_bass_kernel_spmd(nc, in_maps, core_ids=list(range(8)))
    out = np.empty((B, S, D), dtype=np.float32)
    for c in range(8):
        b, rows_idx = core_rows[c]
        xo = np.asarray(res.results[c]["xout"], dtype=np.float32)
        out[b][rows_idx] = xo.transpose(1, 0, 2).reshape(D, TC).T
    return out


# version 12 (final)
# speedup vs baseline: 1.1413x; 1.0464x over previous
import numpy as np
import ml_dtypes
from contextlib import ExitStack

import concourse.bass as bass
import concourse.mybir as mybir
from concourse.bass_utils import run_bass_kernel_spmd

F32 = mybir.dt.float32
BF16 = mybir.dt.bfloat16
U16 = mybir.dt.uint16
U32 = mybir.dt.uint32
U8 = mybir.dt.uint8
ALU = mybir.AluOpType
AF = mybir.ActivationFunctionType

B, S, D, H, E, NM = 4, 4096, 2048, 16, 128, 16384
TC = 2048
TT = 512
NTT = TC // TT
NCH = D // 128
NTN = NM // 128
NGR = 8
NPG = NTN // NGR
SCALE_QK = 1.0 / float(np.sqrt(128.0))
EXP_SHIFT = -25.0
K_G1 = 0.7978845608028654
K_G2 = K_G1 * 0.044715

ISZ = {F32: 4, BF16: 2, U16: 2, U8: 1}


class Ctr:
    def __init__(self, h):
        self.h = h
        self.n = 0


def _carve(arena, off, p, cols, dt):
    nb = cols * ISZ[dt]
    assert off % 64 == 0 and off + nb <= 134 * 1024, (off, nb)
    return arena[0:p, off:off + nb].bitcast(dt)


def build_program(debug=0):
    nc = bass.Bass("TRN2", target_bir_lowering=False)

    x_in = nc.dram_tensor("x", [128, NCH, TC], F32, kind="ExternalInput").ap()
    wq_in = nc.dram_tensor("wq", [H, 128, NCH, E], F32, kind="ExternalInput").ap()
    wk_in = nc.dram_tensor("wk", [H, 128, NCH, E], F32, kind="ExternalInput").ap()
    wv_in = nc.dram_tensor("wv", [H, 128, NCH, E], F32, kind="ExternalInput").ap()
    wo_in = nc.dram_tensor("wo", [128, H, D], F32, kind="ExternalInput").ap()
    u16g = nc.dram_tensor("u16g", [NCH, NGR, 128, 512], U16, kind="ExternalInput").ap()
    u16u = nc.dram_tensor("u16u", [NCH, NGR, 128, 512], U16, kind="ExternalInput").ap()
    u16d = nc.dram_tensor("u16d", [128, 128, 512], U16, kind="ExternalInput").ap()
    sgT = nc.dram_tensor("sgT", [64, NM], F32, kind="ExternalInput").ap()
    suT = nc.dram_tensor("suT", [64, NM], F32, kind="ExternalInput").ap()
    sdT = nc.dram_tensor("sdT", [512, D], F32, kind="ExternalInput").ap()
    kvec_in = nc.dram_tensor("kvec", [128, 8], F32, kind="ExternalInput").ap()

    xout = nc.dram_tensor("xout", [128, NCH, TC], F32, kind="ExternalOutput").ap()

    k1 = "Internal" if debug < 1 else "ExternalOutput"
    k2 = "Internal" if debug < 2 else "ExternalOutput"
    wg_d = nc.dram_tensor("wg_d", [NTN, 128, NCH, 128], BF16, kind=k2).ap()
    wu_d = nc.dram_tensor("wu_d", [NTN, 128, NCH, 128], BF16, kind=k2).ap()
    wd_d = nc.dram_tensor("wd_d", [NGR, 4, 128, NPG, 512], BF16, kind=k2).ap()
    x1_d = nc.dram_tensor("x1_d", [128, NCH, TC], F32, kind=k1).ap()
    x2_d = nc.dram_tensor("x2_d", [128, NCH, TC], F32, kind=k1).ap()
    qf_d = nc.dram_tensor("qf_d", [H, E, TC], BF16, kind=k2).ap()
    kv_loc = nc.dram_tensor("kv_loc", [2 * D, TC], BF16, kind="Internal").ap()
    kv_ag = nc.dram_tensor("kv_ag", [4 * D, TC], BF16, kind="Internal").ap()
    if debug >= 2:
        kvl_dump = nc.dram_tensor("kvl_dump", [2 * D, TC], BF16, kind="ExternalOutput").ap()
        kvag_dump = nc.dram_tensor("kvag_dump", [4 * D, TC], BF16, kind="ExternalOutput").ap()

    with ExitStack() as ctx:
        EN = ctx.enter_context

        xres = EN(nc.sbuf_tensor("xres", [128, NCH * TT], F32))
        xln = EN(nc.sbuf_tensor("xln", [128, NCH * TT], BF16))
        masks = EN(nc.sbuf_tensor("masks", [128, 8 * TT], BF16))
        iq = EN(nc.sbuf_tensor("iq", [128, TT], F32))
        kvec = EN(nc.sbuf_tensor("kvec_sb", [128, 8], F32))
        pmi = EN(nc.sbuf_tensor("pmi", [128, 128], F32))
        identf = EN(nc.sbuf_tensor("identf", [128, 128], F32))
        ones_col = EN(nc.sbuf_tensor("ones_col", [128, 1], BF16))
        ones1 = EN(nc.sbuf_tensor("ones1", [1, 128], BF16))
        c_m25 = EN(nc.sbuf_tensor("c_m25", [128, 1], F32))
        c_eps = EN(nc.sbuf_tensor("c_eps", [1, 1], F32))
        rows = EN(nc.sbuf_tensor("rows", [1, 1536], F32))
        rowsb = EN(nc.sbuf_tensor("rowsb", [1, 1024], BF16))
        arena = EN(nc.sbuf_tensor("arena", [128, 134 * 1024], U8))

        u16st = [_carve(arena, 0 + i * 1024, 128, 512, U16) for i in range(4)]
        sclf = [_carve(arena, 4096 + i * 8192, 128, 2048, F32) for i in range(4)]
        sclb = _carve(arena, 36864, 128, 2048, BF16)
        xfull = _carve(arena, 40960, 128, 2048, U16)
        dsg = _carve(arena, 45056, 128, 2048, U16)
        dt1 = _carve(arena, 49152, 128, 2048, BF16)
        decb = [_carve(arena, 53248 + i * 4096, 128, 2048, BF16) for i in range(4)]

        wgb = [_carve(arena, 0 + i * 4096, 128, 2048, BF16) for i in range(2)]
        wub = [_carve(arena, 8192 + i * 4096, 128, 2048, BF16) for i in range(2)]
        wdb = [_carve(arena, 16384 + i * 16384, 128, 8192, BF16) for i in range(2)]
        inter = _carve(arena, 49152, 128, NPG * TT, BF16)
        dacc = _carve(arena, 65536, 128, 4 * D, F32)
        xball = _carve(arena, 98304, 128, NCH * TT, BF16)
        sqs = [_carve(arena, 114688 + i * 1024, 128, 512, BF16) for i in range(2)]
        gsq = [_carve(arena, 116736 + i * 2048, 128, 512, F32) for i in range(2)]
        gav = _carve(arena, 120832, 128, 512, F32)
        ginn = [_carve(arena, 122880 + i * 2048, 128, 512, F32) for i in range(2)]
        gtha = [_carve(arena, 126976 + i * 2048, 128, 512, F32) for i in range(2)]
        uh = [_carve(arena, 131072 + i * 2048, 128, 512, F32) for i in range(2)]
        mb_sb = _carve(arena, 135168, 128, 512, BF16)
        rb_sb = _carve(arena, 136192, 128, 512, BF16)

        wqb = [_carve(arena, 0 + i * 4096, 128, 2048, BF16) for i in range(2)]
        wkb = [_carve(arena, 8192 + i * 4096, 128, 2048, BF16) for i in range(2)]
        wvb = [_carve(arena, 16384 + i * 4096, 128, 2048, BF16) for i in range(2)]
        qkt = [_carve(arena, 24576 + i * 1024, 128, 512, BF16) for i in range(6)]

        qsb = [_carve(arena, 0 + i * 1024, 128, 512, BF16) for i in range(2)]
        ksb = [_carve(arena, 2048 + i * 8192, 128, 4096, BF16) for i in range(2)]
        vsb = [_carve(arena, 18432 + i * 8192, 128, 4096, BF16) for i in range(2)]
        ptr_ = [_carve(arena, 64512 + i * 1024, 128, 512, BF16) for i in range(4)]
        ptm = [_carve(arena, 68608 + i * 1024, 128, 512, BF16) for i in range(4)]
        zall = _carve(arena, 38912, 128, H * TT, BF16)
        wob = [_carve(arena, 55296 + i * 4096, 128, 2048, BF16) for i in range(2)]
        rb2 = _carve(arena, 63488, 128, 512, BF16)

        bank = [EN(nc.psum_tensor(f"bk{i}", [128, 512], F32)) for i in range(8)]

        names = ["g0", "g1", "v0", "d_u0", "d_u1", "d_u2", "d_u3", "d_s0", "d_s1", "d_s2", "d_s3", "d_st0", "d_st1", "d_st2", "d_st3", "d_v",
                 "f_x", "f_lnv", "f_lnmm", "f_st1", "f_st2", "f_st3",
                 "f_bc", "f_ap", "f_w", "f_gmm", "f_sqa", "f_inn",
                 "f_tha", "f_epi", "f_dw", "f_dpe", "f_dadd", "f_tp",
                 "f_res", "f_xs",
                 "q_w", "q_mm", "q_ev", "q_st0", "q_st1", "c_ag",
                 "a_ld", "a_s", "a_exp", "a_msk", "a_pv", "a_den",
                 "a_rb", "a_rbe", "a_z", "a_wo", "a_omm", "a_res"]
        C = {nm: Ctr(EN(nc.semaphore(nm))) for nm in names}

        Q = {"sy": [], "gp": [], "ve": [], "sc": [], "te": []}

        def em(qn, fn):
            Q[qn].append(fn)

        def wt(qn, ctr, val):
            if val > 0:
                em(qn, lambda e, s=ctr.h, v=val: e.wait_ge(s, v))

        em("gp", lambda g: g.dma_start(out=kvec[:], in_=kvec_in).then_inc(C["g1"].h, 16))
        C["g1"].n += 16
        em("gp", lambda g: g.iota(iq[:], pattern=[[1, TT]], base=0, channel_multiplier=0,
                                  allow_small_or_imprecise_dtypes=True).then_inc(C["g0"].h, 1))
        C["g0"].n += 1
        em("gp", lambda g: g.iota(pmi[:], pattern=[[-1, 128]], base=0, channel_multiplier=1,
                                  allow_small_or_imprecise_dtypes=True).then_inc(C["g0"].h, 1))
        C["g0"].n += 1
        wt("ve", C["g0"], C["g0"].n)
        wt("ve", C["g1"], C["g1"].n)
        for i in range(8):
            em("ve", lambda v, i=i: v.tensor_scalar(
                masks[:, i * TT:(i + 1) * TT], iq[:], kvec[:, i:i + 1], None, op0=ALU.is_ge))
        em("ve", lambda v: v.tensor_scalar(identf[:], pmi[:], 0.0, None, op0=ALU.is_equal))
        em("ve", lambda v: v.memset(ones_col[:], 1.0))
        em("ve", lambda v: v.memset(ones1[:], 1.0))
        em("ve", lambda v: v.memset(c_m25[:], EXP_SHIFT))
        em("ve", lambda v: v.memset(c_eps[:], 1e-5).then_inc(C["v0"].h, 1))
        C["v0"].n += 1
        wt("sc", C["v0"], 1)
        wt("te", C["v0"], 1)

        blocks = []
        for G in range(NGR):
            blocks += [("g", c, G) for c in range(NCH)]
            blocks += [("u", c, G) for c in range(NCH)]
            blocks += [("d", G * NPG + j, G) for j in range(NPG)]
        NBLK = len(blocks)

        for k, (m, a, G) in enumerate(blocks):
            m_idx = {"g": 0, "u": 1, "d": 2}[m]
            if k >= 4:
                wt("sy", C["d_v"], k - 3)
            src = (u16g[a, G] if m == "g" else u16u[a, G] if m == "u" else u16d[a])
            em("sy", lambda e, o=u16st[k % 4], i=src, k=k:
               e.dma_start(out=o, in_=i).then_inc(C[f"d_u{k % 4}"].h, 16))
            C[f"d_u{k % 4}"].n += 16
            c_in = a % NPG if m == "d" else a
            if m == "g":
                ssrc = sgT[4 * c_in:4 * c_in + 4, G * 2048:(G + 1) * 2048]
            elif m == "u":
                ssrc = suT[4 * c_in:4 * c_in + 4, G * 2048:(G + 1) * 2048]
            else:
                ssrc = sdT[4 * a:4 * a + 4, :]
            if k >= 4:
                wt("gp", C["d_v"], k - 3)
            em("gp", lambda e, o=sclf[k % 4], i=ssrc, k=k:
               e.dma_start(out=o, in_=i.unsqueeze(1).broadcast_to([4, 32, 2048])).then_inc(C[f"d_s{k % 4}"].h, 16))
            C[f"d_s{k % 4}"].n += 16
            wt("ve", C[f"d_u{k % 4}"], 16 * (k // 4 + 1))
            u32s = u16st[k % 4].bitcast(U32)
            xf32 = xfull.bitcast(U32)
            sg32v = dsg.bitcast(U32)
            em("ve", lambda v, u=u32s: v.tensor_scalar(xf32[:, 0:256], u, 6, 0x01C001C0,
                                                       op0=ALU.logical_shift_left, op1=ALU.bitwise_and))
            em("ve", lambda v, u=u32s: v.tensor_scalar(xf32[:, 256:512], u, 2, 0x01C001C0,
                                                       op0=ALU.logical_shift_left, op1=ALU.bitwise_and))
            em("ve", lambda v, u=u32s: v.tensor_scalar(xf32[:, 512:768], u, 2, 0x01C001C0,
                                                       op0=ALU.logical_shift_right, op1=ALU.bitwise_and))
            em("ve", lambda v, u=u32s: v.tensor_scalar(xf32[:, 768:1024], u, 6, 0x01C001C0,
                                                       op0=ALU.logical_shift_right, op1=ALU.bitwise_and))
            em("ve", lambda v, u=u32s: v.tensor_scalar(sg32v[:, 0:256], u, 12, 0x80008000,
                                                       op0=ALU.logical_shift_left, op1=ALU.bitwise_and))
            em("ve", lambda v, u=u32s: v.tensor_scalar(sg32v[:, 256:512], u, 8, 0x80008000,
                                                       op0=ALU.logical_shift_left, op1=ALU.bitwise_and))
            em("ve", lambda v, u=u32s: v.tensor_scalar(sg32v[:, 512:768], u, 4, 0x80008000,
                                                       op0=ALU.logical_shift_left, op1=ALU.bitwise_and))
            em("ve", lambda v, u=u32s: v.tensor_scalar(sg32v[:, 768:1024], u, 0, 0x80008000,
                                                       op0=ALU.logical_shift_left, op1=ALU.bitwise_and))
            em("ve", lambda v: v.tensor_scalar(xfull[:], xfull[:], 0x3F00, None, op0=ALU.add))
            em("ve", lambda v: v.tensor_scalar(dt1[:], xfull[:].bitcast(BF16), 2.0, -1.0,
                                               op0=ALU.mult, op1=ALU.add))
            em("ve", lambda v: v.tensor_tensor(dt1[:], xfull[:].bitcast(BF16), dt1[:], op=ALU.min))
            wt("ve", C[f"d_s{k % 4}"], 16 * (k // 4 + 1))
            em("ve", lambda v, s=sclf[k % 4]: v.tensor_copy(sclb[:], s))
            em("ve", lambda v: v.tensor_tensor(
                dsg[:].bitcast(U32), sclb[:].bitcast(U32), dsg[:].bitcast(U32), op=ALU.bitwise_or))
            if k >= 4:
                wt("ve", C[f"d_st{k % 4}"], 16 * (k // 4))
            em("ve", lambda v, o=decb[k % 4]: v.tensor_tensor(
                o, dt1[:], dsg[:].bitcast(BF16), op=ALU.mult).then_inc(C["d_v"].h, 1))
            C["d_v"].n += 1
            wt("sc", C["d_v"], k + 1)
            if m in ("g", "u"):
                dst = (wg_d if m == "g" else wu_d)[G * NPG:(G + 1) * NPG, :, a, :] \
                    .rearrange("a p b -> p a b")
            else:
                dst = wd_d[a // NPG, :, :, a % NPG, :].rearrange("a p b -> p a b")
            em("sc", lambda e, o=dst, i=decb[k % 4], k=k:
               e.dma_start(out=o, in_=i).then_inc(C[f"d_st{k % 4}"].h, 16))
            C[f"d_st{k % 4}"].n += 16

        def ln_tile():
            wt("ve", C["f_x"], C["f_x"].n)
            wt("ve", C["f_lnmm"], C["f_lnmm"].n)
            for c in range(NCH):
                em("ve", lambda v, c=c: v.tensor_copy(
                    xball[:, c * TT:(c + 1) * TT], xres[:, c * TT:(c + 1) * TT]))
                if C["f_lnmm"].n >= 1:
                    wt("ve", C["f_lnmm"], C["f_lnmm"].n - 1)
                em("ve", lambda v, c=c: v.tensor_tensor(
                    sqs[c % 2], xball[:, c * TT:(c + 1) * TT],
                    xball[:, c * TT:(c + 1) * TT], op=ALU.mult).then_inc(C["f_lnv"].h, 1))
                C["f_lnv"].n += 1
                wt("te", C["f_lnv"], C["f_lnv"].n)
                if c == 0:
                    wt("te", C["f_res"], C["f_res"].n)
                    wt("te", C["a_res"], C["a_res"].n)
                em("te", lambda t, c=c: t.matmul(
                    bank[6][0:1, :], ones_col[:], xball[:, c * TT:(c + 1) * TT],
                    start=(c == 0), stop=(c == NCH - 1)))
                em("te", lambda t, c=c: t.matmul(
                    bank[7][0:1, :], ones_col[:], sqs[c % 2],
                    start=(c == 0), stop=(c == NCH - 1)).then_inc(C["f_lnmm"].h, 1))
                C["f_lnmm"].n += 1
            wt("ve", C["f_lnmm"], C["f_lnmm"].n)
            em("ve", lambda v: v.tensor_scalar(rows[:, 0:512], bank[6][0:1, :], 1.0 / 2048.0, None, op0=ALU.mult))
            em("ve", lambda v: v.tensor_scalar(rows[:, 512:1024], bank[7][0:1, :], 1.0 / 2048.0, None, op0=ALU.mult))
            em("ve", lambda v: v.tensor_tensor(rows[:, 1024:1536], rows[:, 0:512], rows[:, 0:512], op=ALU.mult))
            em("ve", lambda v: v.tensor_tensor(
                rows[:, 512:1024], rows[:, 512:1024], rows[:, 1024:1536], op=ALU.subtract)
                .then_inc(C["f_st1"].h, 1))
            C["f_st1"].n += 1
            wt("sc", C["f_st1"], C["f_st1"].n)
            em("sc", lambda s: s.activation(rows[:, 1024:1536], rows[:, 512:1024], AF.Sqrt,
                                            bias=c_eps[:], scale=1.0).then_inc(C["f_st2"].h, 1))
            C["f_st2"].n += 1
            wt("ve", C["f_st2"], C["f_st2"].n)
            em("ve", lambda v: v.reciprocal(rows[:, 512:1024], rows[:, 1024:1536]))
            em("ve", lambda v: v.tensor_copy(rowsb[:, 0:512], rows[:, 0:512]))
            em("ve", lambda v: v.tensor_copy(rowsb[:, 512:1024], rows[:, 512:1024]).then_inc(C["f_st3"].h, 1))
            C["f_st3"].n += 1
            wt("te", C["f_st3"], C["f_st3"].n)
            em("te", lambda t: t.matmul(bank[6][:], ones1[:], rowsb[:, 0:512], start=True, stop=True))
            em("te", lambda t: t.matmul(bank[7][:], ones1[:], rowsb[:, 512:1024], start=True, stop=True)
               .then_inc(C["f_bc"].h, 1))
            C["f_bc"].n += 1
            wt("ve", C["f_bc"], C["f_bc"].n)
            em("ve", lambda v: v.tensor_copy(mb_sb[:], bank[6][:]))
            em("ve", lambda v: v.tensor_copy(rb_sb[:], bank[7][:]))
            wt("ve", C["f_gmm"], C["f_gmm"].n)
            wt("ve", C["q_mm"], C["q_mm"].n)
            for c in range(NCH):
                em("ve", lambda v, c=c: v.tensor_tensor(
                    xball[:, c * TT:(c + 1) * TT], xball[:, c * TT:(c + 1) * TT], mb_sb[:], op=ALU.subtract))
                em("ve", lambda v, c=c: v.tensor_tensor(
                    xln[:, c * TT:(c + 1) * TT], xball[:, c * TT:(c + 1) * TT], rb_sb[:], op=ALU.mult)
                    .then_inc(C["f_ap"].h, 1))
                C["f_ap"].n += 1

        def ffn_pass(src_ap, dst_ap):
            for ti in range(NTT):
                wt("gp", C["f_xs"], C["f_xs"].n)
                em("gp", lambda e, i=src_ap[:, :, ti * TT:(ti + 1) * TT]:
                   e.dma_start(out=xres[:], in_=i).then_inc(C["f_x"].h, 16))
                C["f_x"].n += 16
                ln_tile()
                for g in range(NGR):
                    for j in range(NPG):
                        jn = g * NPG + j
                        sl = jn % 2
                        wq_ = "sy" if sl == 0 else "gp"
                        if C["f_gmm"].n >= 1:
                            wt(wq_, C["f_gmm"], C["f_gmm"].n - 1)
                        em(wq_, lambda e, i=wg_d[jn], o=wgb[sl]: e.dma_start(
                            out=o, in_=i).then_inc(C["f_w"].h, 16))
                        C["f_w"].n += 16
                        em(wq_, lambda e, i=wu_d[jn], o=wub[sl]: e.dma_start(
                            out=o, in_=i).then_inc(C["f_w"].h, 16))
                        C["f_w"].n += 16
                        wt("te", C["f_w"], C["f_w"].n)
                        wt("te", C["f_ap"], C["f_ap"].n)
                        if C["f_epi"].n >= 1:
                            wt("te", C["f_epi"], C["f_epi"].n - 1)
                        for c in range(NCH):
                            em("te", lambda t, c=c, sl=sl: t.matmul(
                                bank[sl][:], wgb[sl][:, c * 128:(c + 1) * 128],
                                xln[:, c * TT:(c + 1) * TT], start=(c == 0), stop=(c == NCH - 1)))
                        for c in range(NCH):
                            if c == NCH - 1:
                                em("te", lambda t, c=c, sl=sl: t.matmul(
                                    bank[2 + sl][:], wub[sl][:, c * 128:(c + 1) * 128],
                                    xln[:, c * TT:(c + 1) * TT], start=False, stop=True)
                                    .then_inc(C["f_gmm"].h, 1))
                            else:
                                em("te", lambda t, c=c, sl=sl: t.matmul(
                                    bank[2 + sl][:], wub[sl][:, c * 128:(c + 1) * 128],
                                    xln[:, c * TT:(c + 1) * TT], start=(c == 0), stop=False))
                        C["f_gmm"].n += 1
                        wt("sc", C["f_gmm"], C["f_gmm"].n)
                        em("sc", lambda s, sl=sl: s.activation(gsq[sl], bank[sl][:], AF.Square))
                        em("sc", lambda s, sl=sl: s.activation(uh[sl], bank[2 + sl][:], AF.Copy, scale=0.5)
                           .then_inc(C["f_sqa"].h, 1))
                        C["f_sqa"].n += 1
                        wt("ve", C["f_sqa"], C["f_sqa"].n)
                        if g == 0 and j == 0:
                            wt("ve", C["f_dpe"], C["f_dpe"].n)
                            wt("ve", C["a_omm"], C["a_omm"].n)
                        if j == 0:
                            wt("ve", C["f_dpe"], C["f_dpe"].n)
                        em("ve", lambda v, sl=sl: v.tensor_scalar(gav[:], gsq[sl], K_G2, K_G1,
                                                                  op0=ALU.mult, op1=ALU.add))
                        em("ve", lambda v, sl=sl: v.tensor_tensor(ginn[sl], gav[:], bank[sl][:], op=ALU.mult)
                           .then_inc(C["f_inn"].h, 1))
                        C["f_inn"].n += 1
                        wt("sc", C["f_inn"], C["f_inn"].n)
                        em("sc", lambda s, sl=sl: s.activation(gtha[sl], ginn[sl], AF.Tanh)
                           .then_inc(C["f_tha"].h, 1))
                        C["f_tha"].n += 1
                        wt("ve", C["f_tha"], C["f_tha"].n)
                        em("ve", lambda v, sl=sl: v.tensor_tensor(gav[:], bank[sl][:], uh[sl], op=ALU.mult))
                        em("ve", lambda v, sl=sl, j=j: v.scalar_tensor_tensor(
                            inter[:, j * TT:(j + 1) * TT], gtha[sl], 1.0, gav[:],
                            op0=ALU.add, op1=ALU.mult).then_inc(C["f_epi"].h, 1))
                        C["f_epi"].n += 1
                    for dt in range(4):
                        dsl = dt % 2
                        dq_ = "sy" if dsl == 0 else "gp"
                        if C["f_dpe"].n >= 4:
                            wt(dq_, C["f_dpe"], C["f_dpe"].n - 4)
                        em(dq_, lambda e, i=wd_d[g, dt], o=wdb[dsl]: e.dma_start(
                            out=o, in_=i).then_inc(C["f_dw"].h, 16))
                        C["f_dw"].n += 16
                        wt("te", C["f_dw"], C["f_dw"].n)
                        wt("te", C["f_epi"], C["f_epi"].n)
                        for tsub in range(4):
                            bsl = 4 + (tsub % 2)
                            if C["f_dadd"].n >= 1:
                                wt("te", C["f_dadd"], C["f_dadd"].n - 1)
                            for j in range(NPG):
                                if j == NPG - 1:
                                    em("te", lambda t, j=j, tsub=tsub, dsl=dsl, bsl=bsl: t.matmul(
                                        bank[bsl][:],
                                        inter[:, j * TT + tsub * 128: j * TT + tsub * 128 + 128],
                                        wdb[dsl][:, j * 512:(j + 1) * 512],
                                        start=False, stop=True).then_inc(C["f_dpe"].h, 1))
                                else:
                                    em("te", lambda t, j=j, tsub=tsub, dsl=dsl, bsl=bsl: t.matmul(
                                        bank[bsl][:],
                                        inter[:, j * TT + tsub * 128: j * TT + tsub * 128 + 128],
                                        wdb[dsl][:, j * 512:(j + 1) * 512],
                                        start=(j == 0), stop=False))
                            C["f_dpe"].n += 1
                            wt("ve", C["f_dpe"], C["f_dpe"].n)
                            if g == 0:
                                wt("ve", C["f_tp"], C["f_tp"].n)
                            dslice = dacc[:, tsub * D + dt * 512: tsub * D + dt * 512 + 512]
                            if g == 0:
                                em("ve", lambda v, o=dslice, b=bank[bsl]: v.tensor_copy(o, b[:])
                                   .then_inc(C["f_dadd"].h, 1))
                            else:
                                em("ve", lambda v, o=dslice, b=bank[bsl]: v.tensor_tensor(
                                    o, o, b[:], op=ALU.add).then_inc(C["f_dadd"].h, 1))
                            C["f_dadd"].n += 1
                wt("te", C["f_dadd"], C["f_dadd"].n)
                wt("te", C["f_ap"], C["f_ap"].n)
                for tsub in range(4):
                    for c in range(NCH):
                        tb_ = 6 + ((tsub * NCH + c) % 2)
                        if C["f_tp"].n >= 2:
                            wt("te", C["f_res"], C["f_tp"].n - 1)
                        em("te", lambda t, tsub=tsub, c=c, tb_=tb_: t.transpose(
                            bank[tb_][:, 0:128],
                            dacc[:, tsub * D + c * 128: tsub * D + c * 128 + 128], identf[:])
                            .then_inc(C["f_tp"].h, 1))
                        C["f_tp"].n += 1
                        wt("ve", C["f_tp"], C["f_tp"].n)
                        xslice = xres[:, c * TT + tsub * 128: c * TT + tsub * 128 + 128]
                        em("ve", lambda v, x=xslice, tb_=tb_: v.scalar_tensor_tensor(
                            x, bank[tb_][:, 0:128], 0.5, x, op0=ALU.mult, op1=ALU.add)
                            .then_inc(C["f_res"].h, 1))
                        C["f_res"].n += 1
                wt("gp", C["f_res"], C["f_res"].n)
                em("gp", lambda e, o=dst_ap[:, :, ti * TT:(ti + 1) * TT]:
                   e.dma_start(out=o, in_=xres[:]).then_inc(C["f_xs"].h, 16))
                C["f_xs"].n += 16

        def p3():
            wt("gp", C["f_gmm"], C["f_gmm"].n)
            wt("gp", C["f_dpe"], C["f_dpe"].n)
            wt("ve", C["f_dpe"], C["f_dpe"].n)
            qmm_hist = []
            for ti in range(NTT):
                wt("gp", C["f_xs"], C["f_xs"].n)
                wt("gp", C["f_ap"], C["f_ap"].n)
                em("gp", lambda e, i=x1_d[:, :, ti * TT:(ti + 1) * TT]:
                   e.dma_start(out=xres[:], in_=i).then_inc(C["f_x"].h, 16))
                C["f_x"].n += 16
                ln_tile()
                for h in range(H):
                    hsl = h % 2
                    gi = ti * H + h
                    if gi >= 2:
                        wt("gp", C["q_mm"], qmm_hist[gi - 2])
                    for (w_in, buf) in ((wq_in, wqb), (wk_in, wkb), (wv_in, wvb)):
                        em("gp", lambda e, i=w_in[h], o=buf[hsl]: e.dma_start(
                            out=o, in_=i).then_inc(C["q_w"].h, 16))
                        C["q_w"].n += 16
                    wt("te", C["q_w"], C["q_w"].n)
                    wt("te", C["f_ap"], C["f_ap"].n)
                    if gi >= 2:
                        wt("te", C["q_ev"], C["q_ev"].n - 1)
                    for c in range(NCH):
                        em("te", lambda t, c=c, hsl=hsl: t.matmul(
                            bank[3 * hsl][:], wqb[hsl][:, c * 128:(c + 1) * 128],
                            xln[:, c * TT:(c + 1) * TT], start=(c == 0), stop=(c == NCH - 1)))
                    for c in range(NCH):
                        em("te", lambda t, c=c, hsl=hsl: t.matmul(
                            bank[3 * hsl + 1][:], wkb[hsl][:, c * 128:(c + 1) * 128],
                            xln[:, c * TT:(c + 1) * TT], start=(c == 0), stop=(c == NCH - 1)))
                    for c in range(NCH):
                        for ks in range(4):
                            last = (c == NCH - 1 and ks == 3)
                            if last:
                                em("te", lambda t, c=c, ks=ks, hsl=hsl: t.matmul(
                                    bank[3 * hsl + 2][:, ks * 128:(ks + 1) * 128],
                                    xln[:, c * TT + ks * 128: c * TT + ks * 128 + 128],
                                    wvb[hsl][:, c * 128:(c + 1) * 128],
                                    start=False, stop=True).then_inc(C["q_mm"].h, 1))
                            else:
                                em("te", lambda t, c=c, ks=ks, hsl=hsl: t.matmul(
                                    bank[3 * hsl + 2][:, ks * 128:(ks + 1) * 128],
                                    xln[:, c * TT + ks * 128: c * TT + ks * 128 + 128],
                                    wvb[hsl][:, c * 128:(c + 1) * 128],
                                    start=(c == 0 and ks == 0), stop=False))
                    C["q_mm"].n += 1
                    qmm_hist.append(C["q_mm"].n)
                    wt("ve", C["q_mm"], C["q_mm"].n)
                    if gi >= 2:
                        wt("ve", C[f"q_st{gi % 2}"], 48 * (gi // 2))
                    em("ve", lambda v, hsl=hsl: v.tensor_copy(qkt[3 * hsl][:], bank[3 * hsl][:]))
                    em("ve", lambda v, hsl=hsl: v.tensor_copy(qkt[3 * hsl + 1][:], bank[3 * hsl + 1][:]))
                    em("ve", lambda v, hsl=hsl: v.tensor_copy(qkt[3 * hsl + 2][:], bank[3 * hsl + 2][:])
                       .then_inc(C["q_ev"].h, 1))
                    C["q_ev"].n += 1
                    wt("sy", C["q_ev"], C["q_ev"].n)
                    em("sy", lambda e, o=qf_d[h, :, ti * TT:(ti + 1) * TT], i=qkt[3 * hsl][:], gi=gi:
                       e.dma_start(out=o, in_=i).then_inc(C[f"q_st{gi % 2}"].h, 16))
                    C[f"q_st{gi % 2}"].n += 16
                    em("sy", lambda e, o=kv_loc[h * 128:(h + 1) * 128, ti * TT:(ti + 1) * TT],
                       i=qkt[3 * hsl + 1][:], gi=gi: e.dma_start(out=o, in_=i).then_inc(C[f"q_st{gi % 2}"].h, 16))
                    C[f"q_st{gi % 2}"].n += 16
                    vdst = kv_loc[D + ti * TT: D + (ti + 1) * TT, h * 128:(h + 1) * 128] \
                        .rearrange("(a p) e -> p a e", p=128)
                    em("sy", lambda e, o=vdst, i=qkt[3 * hsl + 2][:], gi=gi:
                       e.dma_start(out=o, in_=i).then_inc(C[f"q_st{gi % 2}"].h, 16))
                    C[f"q_st{gi % 2}"].n += 16

        def p4():
            wt("gp", C["q_st0"], C["q_st0"].n)
            wt("gp", C["q_st1"], C["q_st1"].n)
            em("gp", lambda g: g.collective_compute(
                "AllGather", ALU.bypass,
                replica_groups=[[0, 1], [2, 3], [4, 5], [6, 7]],
                ins=[kv_loc.opt()],
                outs=[kv_ag.opt()],
            ).then_inc(C["c_ag"].h, 1))
            C["c_ag"].n += 1
            if debug >= 2:
                wt("gp", C["c_ag"], C["c_ag"].n)
                em("gp", lambda g: g.dma_start(out=kvl_dump, in_=kv_loc).then_inc(C["g0"].h, 16))
                C["g0"].n += 16
                em("gp", lambda g: g.dma_start(out=kvag_dump, in_=kv_ag).then_inc(C["g0"].h, 16))
                C["g0"].n += 16
                wt("gp", C["g0"], C["g0"].n)

        def p5():
            wt("sy", C["c_ag"], C["c_ag"].n)
            wt("sy", C["q_mm"], C["q_mm"].n)
            wt("ve", C["f_dpe"], C["f_dpe"].n)
            pv_hist = []
            omm_hist = []
            for j in range(NTT):
                wt("gp", C["a_res"], C["a_res"].n)
                wt("gp", C["f_xs"], C["f_xs"].n)
                em("gp", lambda e, i=x1_d[:, :, j * TT:(j + 1) * TT]:
                   e.dma_start(out=xres[:], in_=i).then_inc(C["f_x"].h, 16))
                C["f_x"].n += 16
                x_mark = C["f_x"].n
                ext = 8 * j + 8
                nkc = (j + 1) * 512
                for h in range(H):
                    hsl = h % 2
                    gh = j * H + h
                    if gh >= 2:
                        wt("sy", C["a_pv"], pv_hist[gh - 2])
                    em("sy", lambda e, hsl=hsl, h=h, j=j:
                       e.dma_start(out=qsb[hsl][:], in_=qf_d[h, :, j * TT:(j + 1) * TT])
                       .then_inc(C["a_ld"].h, 16))
                    C["a_ld"].n += 16
                    for r in range(2):
                        ksrc = kv_ag[r * 2 * D + h * 128: r * 2 * D + (h + 1) * 128, 0:nkc]
                        em("sy", lambda e, i=ksrc, hsl=hsl, r=r, nkc=nkc:
                           e.dma_start(out=ksb[hsl][:, r * 2048:r * 2048 + nkc], in_=i)
                           .then_inc(C["a_ld"].h, 16))
                        C["a_ld"].n += 16
                        vsrc = kv_ag[r * 2 * D + D: r * 2 * D + D + nkc, h * 128:(h + 1) * 128] \
                            .rearrange("(a p) e -> p a e", p=128)
                        em("sy", lambda e, i=vsrc, hsl=hsl, r=r, nkc=nkc:
                           e.dma_start(out=vsb[hsl][:, r * 2048:r * 2048 + nkc], in_=i)
                           .then_inc(C["a_ld"].h, 16))
                        C["a_ld"].n += 16
                    msk_at = {}

                    def emit_pv(i, ext=ext, hsl=hsl):
                        masked = i >= ext - 8
                        g = C["a_pv"].n
                        gk = i // 4
                        r, lb, q4 = gk % 2, gk // 2, i % 4
                        vcol = r * 2048 + (lb * 4 + q4) * 128
                        if masked:
                            wt("te", C["a_msk"], msk_at[i])
                            pt = ptm[g % 4]
                        else:
                            wt("te", C["a_exp"], g + 1)
                            pt = ptr_[g % 4]
                        if i == 0:
                            wt("te", C["a_z"], C["a_z"].n)
                            wt("te", C["a_rbe"], C["a_rbe"].n)
                        em("te", lambda t, hsl=hsl, vcol=vcol, pt=pt, i=i, ext=ext: t.matmul(
                            bank[4][:], vsb[hsl][:, vcol:vcol + 128], pt[:],
                            start=(i == 0), stop=(i == ext - 1)))
                        em("te", lambda t, pt=pt, i=i, ext=ext: t.matmul(
                            bank[5][0:1, :], ones_col[:], pt[:],
                            start=(i == 0), stop=(i == ext - 1)).then_inc(C["a_pv"].h, 1))
                        C["a_pv"].n += 1

                    wt("te", C["a_ld"], C["a_ld"].n)
                    for i in range(ext):
                        gk = i // 4
                        r, lb, q4 = gk % 2, gk // 2, i % 4
                        kcol = r * 2048 + lb * 512 + q4 * 128
                        g = C["a_s"].n
                        if g >= 4:
                            wt("te", C["a_exp"], g - 3)
                        em("te", lambda t, hsl=hsl, kcol=kcol, g=g: t.matmul(
                            bank[g % 4][:], ksb[hsl][:, kcol:kcol + 128], qsb[hsl][:],
                            start=True, stop=True).then_inc(C["a_s"].h, 1))
                        C["a_s"].n += 1
                        wt("sc", C["a_s"], C["a_s"].n)
                        if g >= 4:
                            wt("sc", C["a_pv"], g - 3)
                        em("sc", lambda s, g=g: s.activation(
                            ptr_[g % 4][:], bank[g % 4][:], AF.Exp, bias=c_m25[:], scale=SCALE_QK)
                            .then_inc(C["a_exp"].h, 1))
                        C["a_exp"].n += 1
                        if i >= ext - 8:
                            mi = i - (ext - 8)
                            wt("ve", C["a_exp"], C["a_exp"].n)
                            if g >= 4:
                                wt("ve", C["a_pv"], g - 3)
                            em("ve", lambda v, g=g, mi=mi: v.tensor_tensor(
                                ptm[g % 4][:], ptr_[g % 4][:], masks[:, mi * TT:(mi + 1) * TT],
                                op=ALU.mult).then_inc(C["a_msk"].h, 1))
                            C["a_msk"].n += 1
                            msk_at[i] = C["a_msk"].n
                        if i >= 1:
                            emit_pv(i - 1)
                    emit_pv(ext - 1)
                    pv_hist.append(C["a_pv"].n)
                    wt("ve", C["a_pv"], C["a_pv"].n)
                    em("ve", lambda v: v.tensor_scalar(rows[:, 0:512], bank[5][0:1, :], 1.0, None, op0=ALU.mult))
                    em("ve", lambda v: v.reciprocal(rows[:, 512:1024], rows[:, 0:512]))
                    em("ve", lambda v: v.tensor_copy(rowsb[:, 0:512], rows[:, 512:1024])
                       .then_inc(C["a_den"].h, 1))
                    C["a_den"].n += 1
                    wt("te", C["a_den"], C["a_den"].n)
                    em("te", lambda t: t.matmul(bank[5][:], ones1[:], rowsb[:, 0:512], start=True, stop=True)
                       .then_inc(C["a_rb"].h, 1))
                    C["a_rb"].n += 1
                    wt("ve", C["a_rb"], C["a_rb"].n)
                    em("ve", lambda v: v.tensor_copy(rb2[:], bank[5][:]).then_inc(C["a_rbe"].h, 1))
                    C["a_rbe"].n += 1
                    em("ve", lambda v, h=h: v.tensor_tensor(
                        zall[:, h * TT:(h + 1) * TT], bank[4][:], rb2[:], op=ALU.mult)
                        .then_inc(C["a_z"].h, 1))
                    C["a_z"].n += 1
                for c in range(NCH):
                    csl = c % 2
                    gc = j * NCH + c
                    if gc >= 2:
                        wt("gp", C["a_omm"], omm_hist[gc - 2])
                    em("gp", lambda e, i=wo_in[:, :, c * 128:(c + 1) * 128], csl=csl: e.dma_start(
                        out=wob[csl][:], in_=i).then_inc(C["a_wo"].h, 16))
                    C["a_wo"].n += 16
                    wt("te", C["a_wo"], C["a_wo"].n)
                    wt("te", C["a_z"], C["a_z"].n)
                    if gc >= 2:
                        wt("te", C["a_res"], C["a_res"].n - 1)
                    for h in range(H):
                        if h == H - 1:
                            em("te", lambda t, h=h, csl=csl: t.matmul(
                                bank[6 + csl][:], wob[csl][:, h * 128:(h + 1) * 128],
                                zall[:, h * TT:(h + 1) * TT], start=False, stop=True)
                                .then_inc(C["a_omm"].h, 1))
                        else:
                            em("te", lambda t, h=h, csl=csl: t.matmul(
                                bank[6 + csl][:], wob[csl][:, h * 128:(h + 1) * 128],
                                zall[:, h * TT:(h + 1) * TT], start=(h == 0), stop=False))
                    C["a_omm"].n += 1
                    omm_hist.append(C["a_omm"].n)
                    wt("ve", C["a_omm"], C["a_omm"].n)
                    wt("ve", C["f_x"], x_mark)
                    xs = xres[:, c * TT:(c + 1) * TT]
                    em("ve", lambda v, xs=xs, csl=csl: v.tensor_tensor(
                        xs, xs, bank[6 + csl][:], op=ALU.add).then_inc(C["a_res"].h, 1))
                    C["a_res"].n += 1
                wt("gp", C["a_res"], C["a_res"].n)
                em("gp", lambda e, o=x2_d[:, :, j * TT:(j + 1) * TT]:
                   e.dma_start(out=o, in_=xres[:]).then_inc(C["f_xs"].h, 16))
                C["f_xs"].n += 16

        for _nm in ("d_st0", "d_st1", "d_st2", "d_st3"):
            for _q in ("sy", "gp", "te", "ve", "sc"):
                wt(_q, C[_nm], C[_nm].n)
        for _q in ("sy", "gp", "te", "sc"):
            wt(_q, C["d_v"], C["d_v"].n)
        ffn_pass(x_in, x1_d)
        p3()
        p4()
        p5()
        wt("sy", C["a_pv"], C["a_pv"].n)
        wt("sy", C["a_omm"], C["a_omm"].n)
        wt("ve", C["a_omm"], C["a_omm"].n)
        ffn_pass(x2_d, xout)
        wt("sy", C["f_xs"], C["f_xs"].n)

        with nc.Block() as block:
            @block.sync
            def _(e):
                for fn in Q["sy"]:
                    fn(e)

            @block.gpsimd
            def _(e):
                for fn in Q["gp"]:
                    fn(e)

            @block.vector
            def _(e):
                for fn in Q["ve"]:
                    fn(e)

            @block.scalar
            def _(e):
                for fn in Q["sc"]:
                    fn(e)

            @block.tensor
            def _(e):
                for fn in Q["te"]:
                    fn(e)

    return nc


def _host_prepare(inputs):
    x = np.asarray(inputs["L0_attn_x"], dtype=np.float32)
    WQ = np.asarray(inputs["L0_attn_W_Q"], dtype=np.float32)
    WK = np.asarray(inputs["L0_attn_W_K"], dtype=np.float32)
    WV = np.asarray(inputs["L0_attn_W_V"], dtype=np.float32)
    WO = np.asarray(inputs["L0_attn_W_O"], dtype=np.float32)

    def unpack(Wp):
        b = np.asarray(Wp).astype(np.uint8)
        nib = np.empty((b.shape[0], b.shape[1] * 2), np.uint8)
        nib[:, 0::2] = b & 15
        nib[:, 1::2] = (b >> 4) & 15
        return nib

    nibg = unpack(inputs["L0_ffn_gate_W_packed"])
    nibu = unpack(inputs["L0_ffn_up_W_packed"])
    nibd = unpack(inputs["L0_ffn_down_W_packed"])

    def pack_gu(nib):
        A = nib.reshape(NGR, 4, 512, NCH, 128).astype(np.uint16)
        u = A[:, 0] | (A[:, 1] << 4) | (A[:, 2] << 8) | (A[:, 3] << 12)
        return np.ascontiguousarray(u.transpose(2, 0, 3, 1))

    u16g_a = pack_gu(nibg)
    u16u_a = pack_gu(nibu)
    Y = nibd.reshape(4, 512, 128, 128).astype(np.uint16)
    u16d_a = np.ascontiguousarray(
        (Y[0] | (Y[1] << 4) | (Y[2] << 8) | (Y[3] << 12)).transpose(1, 2, 0))

    sgT = np.ascontiguousarray(np.asarray(inputs["L0_ffn_gate_scales"], dtype=np.float32).T)
    suT = np.ascontiguousarray(np.asarray(inputs["L0_ffn_up_scales"], dtype=np.float32).T)
    sdT = np.ascontiguousarray(np.asarray(inputs["L0_ffn_down_scales"], dtype=np.float32).T)

    def _whost(W):
        return np.ascontiguousarray(W.reshape(H, E, NCH, 128).transpose(0, 3, 2, 1))

    wq = _whost(WQ)
    wk = _whost(WK)
    wv = _whost(WV)
    wo = np.ascontiguousarray(WO.transpose(2, 0, 1))

    in_maps = []
    core_rows = []
    for c in range(8):
        b, p = c // 2, c % 2
        gids = [2 * jj + p for jj in range(NTT)]
        rows_idx = np.concatenate([np.arange(g * TT, (g + 1) * TT) for g in gids])
        core_rows.append((b, rows_idx))
        xc = np.ascontiguousarray(x[b][rows_idx].T.reshape(NCH, 128, TC).transpose(1, 0, 2))
        kv = (np.arange(128, dtype=np.float32)[:, None]
              + 128.0 * np.arange(8, dtype=np.float32)[None, :] - 512.0 * p)
        in_maps.append({
            "x": xc, "wq": wq, "wk": wk, "wv": wv, "wo": wo,
            "u16g": u16g_a, "u16u": u16u_a, "u16d": u16d_a,
            "sgT": sgT, "suT": suT, "sdT": sdT,
            "kvec": np.ascontiguousarray(kv),
        })
    return in_maps, core_rows


def kernel(**inputs):
    in_maps, core_rows = _host_prepare(inputs)
    nc = build_program()
    res = run_bass_kernel_spmd(nc, in_maps, core_ids=list(range(8)))
    out = np.empty((B, S, D), dtype=np.float32)
    for c in range(8):
        b, rows_idx = core_rows[c]
        xo = np.asarray(res.results[c]["xout"], dtype=np.float32)
        out[b][rows_idx] = xo.transpose(1, 0, 2).reshape(D, TC).T
    return out
